# version 01
# baseline (speedup 1.0000x reference)
import math
from contextlib import ExitStack

import numpy as np
import ml_dtypes

import concourse.bass as bass
import concourse.mybir as mybir
from concourse.bass_utils import run_bass_kernel_spmd

F32 = mybir.dt.float32
BF16 = mybir.dt.bfloat16
I32 = mybir.dt.int32
U8 = mybir.dt.uint8
AF = mybir.ActivationFunctionType
ALU = mybir.AluOpType
AX = mybir.AxisListType

NCORES = 8
B, S, E = 4, 2048, 4096
TOK = B * S
TPC = TOK // NCORES
NEG = -3.0e38
BF = ml_dtypes.bfloat16

ENGINES = ("tensor", "vector", "scalar", "gpsimd", "sync")


def ceil_div(a, b):
    return (a + b - 1) // b


class Tok:
    __slots__ = ("eng", "cnt")

    def __init__(self, eng, cnt):
        self.eng = eng
        self.cnt = cnt


class Seq:

    NDMA = 12

    def __init__(self, nc, ctx):
        self.nc = nc
        self.ops = {e: [] for e in ENGINES}
        self.counts = {e: 0 for e in ENGINES}
        self.sems = {e: ctx.enter_context(nc.semaphore(f"sem_{e}")) for e in ENGINES}
        for i in range(self.NDMA):
            k = f"dma{i}"
            self.sems[k] = ctx.enter_context(nc.semaphore(k))
            self.counts[k] = 0
        self._dma_rr = 0
        self.last_writer = {}
        self.readers = {}

    def op(self, eng, emit, reads=(), writes=(), extra=(), dma=False, chan=None):
        assert eng in ENGINES, eng
        if dma and eng == "gpsimd":
            eng = "scalar"
        deps = list(extra) if not isinstance(extra, Tok) else [extra]
        for r in reads:
            t = self.last_writer.get(id(r))
            if t is not None:
                deps.append(t)
        for w in writes:
            t = self.last_writer.get(id(w))
            if t is not None:
                deps.append(t)
            deps.extend(self.readers.get(id(w), ()))
        if dma:
            key = chan if chan is not None else f"dma{self._dma_rr % self.NDMA}"
            if chan is None:
                self._dma_rr += 1
            if self.counts[key] > 0:
                deps.append(Tok(key, self.counts[key]))
        else:
            key = eng
        wmax = {}
        for t in deps:
            if t is None:
                continue
            if wmax.get(t.eng, 0) < t.cnt:
                wmax[t.eng] = t.cnt
        self.counts[key] += 16 if dma else 1
        mytok = Tok(key, self.counts[key])
        self.ops[eng].append((wmax, emit, key, 16 if dma else 1))
        for r in reads:
            self.readers.setdefault(id(r), []).append(mytok)
        for w in writes:
            self.last_writer[id(w)] = mytok
            self.readers[id(w)] = []
        return mytok

    def finalize(self, block, final_waits=()):
        sems = self.sems
        fw = {}
        for t in final_waits:
            if fw.get(t.eng, 0) < t.cnt:
                fw[t.eng] = t.cnt

        def mk_section(eng_name):
            def section(engine_obj):
                done = {}
                for wmax, emit, key, inc in self.ops[eng_name]:
                    for e, c in wmax.items():
                        if done.get(e, -1) < c:
                            engine_obj.wait_ge(sems[e], c)
                            done[e] = c
                    inst = emit(engine_obj)
                    assert inst is not None, f"emit returned None on {eng_name}"
                    inst.then_inc(sems[key], inc)
                if eng_name == "sync":
                    for e, c in fw.items():
                        engine_obj.wait_ge(sems[e], c)
            return section

        for e in ENGINES:
            if self.ops[e] or (e == "sync" and fw):
                getattr(block, e)(mk_section(e))


class Builder:

    def __init__(self, name, npsum=8):
        self.nc = bass.Bass("TRN2", target_bir_lowering=False, name=name)
        self.ctx = ExitStack()
        self.seq = None
        self.psum = []
        self.psum_rr = 0
        self._n = 0
        self._npsum = npsum

    def __enter__(self):
        self.ctx.__enter__()
        self.seq = Seq(self.nc, self.ctx)
        self.psum = [self.ctx.enter_context(self.nc.psum_tensor(f"psb{i}", [128, 512], F32))
                     for i in range(self._npsum)]
        return self

    def __exit__(self, *exc):
        return self.ctx.__exit__(*exc)

    def sbuf(self, shape, dt=F32, name=None):
        self._n += 1
        nm = name or f"t{self._n}"
        return self.ctx.enter_context(self.nc.sbuf_tensor(nm, shape, dt))

    def sbufs(self, n, shape, dt=F32, name=None):
        return [self.sbuf(shape, dt, f"{name or 'ts'}{self._n}_{i}") for i in range(n)]

    def take_psum(self):
        p = self.psum[self.psum_rr]
        self.psum_rr = (self.psum_rr + 1) % len(self.psum)
        return p

    def const_tile(self, value, parts=1, dt=F32):
        key = (value, parts, str(dt))
        cache = getattr(self, "_consts", None)
        if cache is None:
            cache = self._consts = {}
        if key not in cache:
            t = self.sbuf([parts, 1], dt, f"c_{len(cache)}")
            self.seq.op("gpsimd", lambda e: e.memset(t[:], value), writes=[t])
            cache[key] = t
        return cache[key]


def ones_consts(b: Builder):
    s = b.seq
    o_col = b.sbuf([128, 1], F32, "ones_col")
    o_row = b.sbuf([1, 128], F32, "ones_row")
    o_rowb = b.sbuf([1, 128], BF16, "ones_rowb")
    o_colb = b.sbuf([128, 1], BF16, "ones_colb")
    s.op("gpsimd", lambda e: e.memset(o_col[:], 1.0), writes=[o_col])
    s.op("gpsimd", lambda e: e.memset(o_row[:], 1.0), writes=[o_row])
    s.op("gpsimd", lambda e: e.memset(o_rowb[:], 1.0), writes=[o_rowb])
    s.op("gpsimd", lambda e: e.memset(o_colb[:], 1.0), writes=[o_colb])
    return o_col, o_row, o_rowb, o_colb


def emit_ln_fm(b: Builder, d_x, col0, xln, half_cols=512, n_chunks=32,
               eps=1e-5, out_dt_note="bf16"):
    s = b.seq
    nc = b.nc
    o_col, o_row, _, _ = b.c_ones
    xa = getattr(b, "_ln_xa", None)
    if xa is None:
        b._ln_xa = b.sbufs(2, [128, half_cols], F32, "ln_xa")
        b._ln_sq = b.sbufs(2, [128, half_cols], F32, "ln_sq")
        b._ln_stat = [b.sbuf([1, half_cols], F32, f"ln_stat{i}") for i in range(3)]
        b._ln_bc = [b.sbuf([128, half_cols], F32, f"ln_bc{i}") for i in range(2)]
    xa = b._ln_xa
    sq = b._ln_sq
    stat_m, stat_r, stat_t = b._ln_stat
    rbc, mbc = b._ln_bc

    ps_s = b.take_psum()
    ps_q = b.take_psum()
    for kc in range(n_chunks):
        xt = xa[kc % 2]
        s.op("sync", lambda e, xt=xt, kc=kc: e.dma_start(
            out=xt[:], in_=d_x[kc * 128:(kc + 1) * 128, col0:col0 + half_cols]),
            reads=[d_x], writes=[xt], dma=True)
        s.op("tensor", lambda e, xt=xt, kc=kc: e.matmul(
            ps_s[0:1, :half_cols], o_col[:], xt[:], start=(kc == 0), stop=(kc == n_chunks - 1)),
            reads=[xt, o_col], writes=[ps_s])
    for kc in range(n_chunks):
        xt = xa[kc % 2]
        st = sq[kc % 2]
        s.op("sync", lambda e, xt=xt, kc=kc: e.dma_start(
            out=xt[:], in_=d_x[kc * 128:(kc + 1) * 128, col0:col0 + half_cols]),
            reads=[d_x], writes=[xt], dma=True)
        s.op("scalar", lambda e, st=st, xt=xt: e.activation(
            out=st[:], in_=xt[:], func=AF.Square), reads=[xt], writes=[st])
        s.op("tensor", lambda e, st=st, kc=kc: e.matmul(
            ps_q[0:1, :half_cols], o_col[:], st[:], start=(kc == 0), stop=(kc == n_chunks - 1)),
            reads=[st, o_col], writes=[ps_q])
    inv_n = 1.0 / (128 * n_chunks)
    s.op("scalar", lambda e: e.activation(
        out=stat_m[:], in_=ps_s[0:1, :half_cols], func=AF.Copy, scale=inv_n),
        reads=[ps_s], writes=[stat_m])
    s.op("vector", lambda e: e.tensor_mul(out=stat_t[:], in0=stat_m[:], in1=stat_m[:]),
         reads=[stat_m], writes=[stat_t])
    s.op("vector", lambda e: e.scalar_tensor_tensor(
        out=stat_r[:], in0=ps_q[0:1, :half_cols], scalar=inv_n, in1=stat_t[:],
        op0=ALU.mult, op1=ALU.subtract), reads=[ps_q, stat_t], writes=[stat_r])
    epst = b.const_tile(eps)
    s.op("scalar", lambda e: e.activation(
        out=stat_r[:], in_=stat_r[:], func=AF.Sqrt, bias=epst[:]),
        reads=[stat_r, epst], writes=[stat_r])
    s.op("vector", lambda e: e.reciprocal(out=stat_r[:], in_=stat_r[:]),
         reads=[stat_r], writes=[stat_r])
    s.op("vector", lambda e: e.tensor_mul(out=stat_t[:], in0=stat_m[:], in1=stat_r[:]),
         reads=[stat_m, stat_r], writes=[stat_t])
    ps_r = b.take_psum()
    ps_m = b.take_psum()
    s.op("tensor", lambda e: e.matmul(ps_r[:, :half_cols], o_row[:], stat_r[:],
                                      start=True, stop=True),
         reads=[o_row, stat_r], writes=[ps_r])
    s.op("tensor", lambda e: e.matmul(ps_m[:, :half_cols], o_row[:], stat_t[:],
                                      start=True, stop=True),
         reads=[o_row, stat_t], writes=[ps_m])
    s.op("vector", lambda e: e.tensor_copy(out=rbc[:], in_=ps_r[:, :half_cols]),
         reads=[ps_r], writes=[rbc])
    s.op("vector", lambda e: e.tensor_copy(out=mbc[:], in_=ps_m[:, :half_cols]),
         reads=[ps_m], writes=[mbc])
    t1 = getattr(b, "_ln_t1", None)
    if t1 is None:
        b._ln_t1 = b.sbufs(2, [128, half_cols], F32, "ln_t1")
    t1 = b._ln_t1
    for kc in range(n_chunks):
        xt = xa[kc % 2]
        tt = t1[kc % 2]
        s.op("sync", lambda e, xt=xt, kc=kc: e.dma_start(
            out=xt[:], in_=d_x[kc * 128:(kc + 1) * 128, col0:col0 + half_cols]),
            reads=[d_x], writes=[xt], dma=True)
        s.op("gpsimd", lambda e, xt=xt, tt=tt: e.tensor_mul(
            out=tt[:], in0=xt[:], in1=rbc[:]), reads=[xt, rbc], writes=[tt])
        s.op("vector", lambda e, tt=tt, kc=kc: e.tensor_sub(
            out=xln[kc][:], in0=tt[:], in1=mbc[:]), reads=[tt, mbc], writes=[xln[kc]])


def emit_rowsum(b: Builder, tiles, out_row, half_cols=512):
    s = b.seq
    o_col = b.c_ones[3]
    ps = b.take_psum()
    n = len(tiles)
    for kc in range(n):
        s.op("tensor", lambda e, kc=kc: e.matmul(
            ps[0:1, :half_cols], o_col[:], tiles[kc][:], start=(kc == 0), stop=(kc == n - 1)),
            reads=[tiles[kc], o_col], writes=[ps])
    s.op("vector", lambda e: e.tensor_copy(out=out_row[:], in_=ps[0:1, :half_cols]),
         reads=[ps], writes=[out_row])



def build_ffn_launch(debug=False):
    b = Builder("ffn")
    nc = b.nc
    d_x = nc.dram_tensor("x", [E, TPC], F32, kind="ExternalInput").ap()
    d_w = nc.dram_tensor("wintT", [E, E], I32, kind="ExternalInput").ap()
    d_bias = nc.dram_tensor("bias", [E], F32, kind="ExternalInput").ap()
    d_sz = nc.dram_tensor("sz", [1, 2], F32, kind="ExternalInput").ap()
    d_out = nc.dram_tensor("out", [E, TPC], F32, kind="ExternalOutput").ap()
    d_wbf = nc.dram_tensor("wbf", [E, E], BF16, kind="Internal").ap()
    if debug:
        d_dbg_xln = nc.dram_tensor("dbg_xln", [128, 512], BF16, kind="ExternalOutput").ap()
        d_dbg_rs = nc.dram_tensor("dbg_rs", [1, 512], BF16, kind="ExternalOutput").ap()
        d_dbg_ep = nc.dram_tensor("dbg_ep", [128, 512], F32, kind="ExternalOutput").ap()
        d_dbg_w = nc.dram_tensor("dbg_w", [128, 4096], BF16, kind="ExternalOutput").ap()
        d_dbg_epall = nc.dram_tensor("dbg_epall", [E, TPC], F32, kind="ExternalOutput").ap()

    with b as b:
        s = b.seq
        b.c_ones = ones_consts(b)
        o_col, o_row, o_rowb, o_colb = b.c_ones

        xln = [[b.sbuf([128, 512], BF16, f"xln{kc}_{h}") for h in range(2)] for kc in range(32)]
        rowsum = [b.sbuf([1, 512], BF16, f"rs{h}") for h in range(2)]
        wint = b.sbufs(2, [128, 4096], I32, "wint")
        wbfs = b.sbufs(2, [128, 4096], BF16, "wbfs")
        wt = b.sbufs(4, [128, 128], BF16, "wt")
        ep = b.sbufs(2, [128, 512], F32, "ep")
        xr = b.sbufs(2, [128, 512], F32, "xr")
        out_t = b.sbufs(2, [128, 512], F32, "outt")
        bias_t = b.sbuf([128, 32], F32, "bias_t")
        szt = b.sbuf([1, 2], F32, "szt")
        scale_bc = b.sbuf([128, 1], F32, "scale_bc")
        negzp = b.sbuf([1, 128], BF16, "negzp")

        with nc.Block() as block:
            s.op("sync", lambda e: e.dma_start(out=szt[:], in_=d_sz), writes=[szt], dma=True)
            for c in range(32):
                s.op("sync", lambda e, c=c: e.dma_start(
                    out=bias_t[:, c:c + 1],
                    in_=d_bias[c * 128:(c + 1) * 128].rearrange("(p o) -> p o", o=1)),
                    reads=[d_bias], writes=[bias_t], dma=True)
            ps0 = b.take_psum()
            s.op("tensor", lambda e: e.matmul(ps0[:, 0:1], o_row[:], szt[0:1, 0:1],
                                              start=True, stop=True),
                 reads=[o_row, szt], writes=[ps0])
            s.op("vector", lambda e: e.tensor_copy(out=scale_bc[:], in_=ps0[:, 0:1]),
                 reads=[ps0], writes=[scale_bc])
            s.op("vector", lambda e: e.tensor_scalar(
                out=negzp[:], in0=o_row[:], scalar1=szt[0:1, 1:2], scalar2=-1.0,
                op0=ALU.mult, op1=ALU.mult), reads=[o_row, szt], writes=[negzp])

            for kc in range(32):
                wi = wint[kc % 2]
                wb = wbfs[kc % 2]
                s.op("sync", lambda e, wi=wi, kc=kc: e.dma_start(
                    out=wi[:], in_=d_w[kc * 128:(kc + 1) * 128, :]),
                    reads=[d_w], writes=[wi], dma=True)
                s.op("vector", lambda e, wi=wi, wb=wb: e.tensor_copy(out=wb[:], in_=wi[:]),
                     reads=[wi], writes=[wb])
                s.op("gpsimd", lambda e, wb=wb, kc=kc: e.dma_start(
                    out=d_wbf[kc * 128:(kc + 1) * 128, :], in_=wb[:]),
                    reads=[wb], writes=[d_wbf], dma=True)

            for h in range(2):
                emit_ln_fm(b, d_x, h * 512, [xln[kc][h] for kc in range(32)])
                emit_rowsum(b, [xln[kc][h] for kc in range(32)], rowsum[h])

            wblk = b.sbufs(2, [128, 4096], BF16, "wblk")
            for oc in range(32):
                wset = wblk[oc % 2]
                s.op("sync", lambda e, wset=wset, oc=oc: e.dma_start(
                    out=wset[:], in_=d_wbf.rearrange("(kc p) m -> p kc m", p=128)[:, :, oc * 128:(oc + 1) * 128]),
                    reads=[d_wbf], writes=[wset], dma=True)
                pss = [b.take_psum(), b.take_psum()]
                for hh in range(2):
                    psx = pss[hh]
                    for kc in range(32):
                        s.op("tensor", lambda e, kc=kc, hh=hh, ps=psx, wset=wset: e.matmul(
                            ps[:], wset[:, kc * 128:(kc + 1) * 128], xln[kc][hh][:],
                            start=(kc == 0), stop=False),
                            reads=[wset, xln[kc][hh]], writes=[psx])
                    s.op("tensor", lambda e, hh=hh, ps=psx: e.matmul(
                        ps[:], negzp[:], rowsum[hh][:], start=False, stop=True),
                        reads=[negzp, rowsum[hh]], writes=[psx])
                for hh in range(2):
                    ept, xrt, ot = ep[hh], xr[hh], out_t[hh]
                    s.op("vector", lambda e, ept=ept, oc=oc, ps=pss[hh]: e.tensor_scalar(
                        out=ept[:], in0=ps[:], scalar1=scale_bc[:],
                        scalar2=bias_t[:, oc:oc + 1], op0=ALU.mult, op1=ALU.add),
                        reads=[pss[hh], scale_bc, bias_t], writes=[ept])
                    s.op("sync", lambda e, xrt=xrt, oc=oc, hh=hh: e.dma_start(
                        out=xrt[:], in_=d_x[oc * 128:(oc + 1) * 128, hh * 512:(hh + 1) * 512]),
                        reads=[d_x], writes=[xrt], dma=True)
                    s.op("vector", lambda e, ot=ot, ept=ept, xrt=xrt: e.tensor_add(
                        out=ot[:], in0=ept[:], in1=xrt[:]), reads=[ept, xrt], writes=[ot])
                    s.op("gpsimd", lambda e, ot=ot, oc=oc, hh=hh: e.dma_start(
                        out=d_out[oc * 128:(oc + 1) * 128, hh * 512:(hh + 1) * 512], in_=ot[:]),
                        reads=[ot], writes=[d_out], dma=True)
                    if debug:
                        s.op("gpsimd", lambda e, ept=ept, oc=oc, hh=hh: e.dma_start(
                            out=d_dbg_epall[oc * 128:(oc + 1) * 128, hh * 512:(hh + 1) * 512], in_=ept[:]),
                            reads=[ept], writes=[d_dbg_epall], dma=True)
            fw = [s.last_writer[id(d_out)]]
            if debug:
                fw.append(s.last_writer[id(d_dbg_epall)])
                fw.append(s.op("sync", lambda e: e.dma_start(out=d_dbg_xln, in_=xln[0][0][:]),
                               reads=[xln[0][0]], dma=True))
                fw.append(s.op("sync", lambda e: e.dma_start(out=d_dbg_rs, in_=rowsum[0][:]),
                               reads=[rowsum[0]], dma=True))
                fw.append(s.op("sync", lambda e: e.dma_start(out=d_dbg_ep, in_=ep[0][:]),
                               reads=[ep[0]], dma=True))
                fw.append(s.op("sync", lambda e: e.dma_start(out=d_dbg_w, in_=d_wbf[3*128:4*128, :]),
                               reads=[d_wbf], dma=True))
            s.finalize(block, final_waits=fw)
    return b.nc


def run_ffn(x_fm, wintT, bias, scale, zp, debug=False):
    nc = build_ffn_launch(debug=debug)
    sz = np.array([[np.float32(scale), np.float32(zp)]], dtype=np.float32)
    in_maps = []
    for c in range(NCORES):
        sl = np.ascontiguousarray(x_fm[:, c * TPC:(c + 1) * TPC])
        in_maps.append({"x": sl, "wintT": wintT, "bias": bias, "sz": sz})
    res = run_bass_kernel_spmd(nc, in_maps, core_ids=list(range(NCORES)))
    out = np.concatenate([res.results[c]["out"] for c in range(NCORES)], axis=1)
    if debug:
        return out, res.results
    return out


def kernel(**inputs):
    raise NotImplementedError("under construction")



def emit_proj_stream(b, d_w, oc_list, rhs_tiles, epilogue, n_k=32, wtag="w",
                     w_dt=F32, extra_group=None):
    s = b.seq
    key = f"_proj_{wtag}"
    bufs = getattr(b, key, None)
    if bufs is None:
        wf = b.sbufs(1, [128, n_k * 128], w_dt, f"{wtag}f")
        wb = b.sbufs(2, [128, n_k * 128], BF16, f"{wtag}b")
        setattr(b, key, (wf, wb))
    else:
        wf, wb = bufs
    d_w_r = d_w.rearrange("(kc p) m -> p kc m", p=128)
    for i, oc in enumerate(oc_list):
        wfx, wbx = wf[0], wb[i % 2]
        s.op("sync", lambda e, wfx=wfx, oc=oc: e.dma_start(
            out=wfx[:], in_=d_w_r[:, :, oc * 128:(oc + 1) * 128]),
            reads=[d_w], writes=[wfx], dma=True)
        s.op("vector", lambda e, wfx=wfx, wbx=wbx: e.tensor_copy(out=wbx[:], in_=wfx[:]),
             reads=[wfx], writes=[wbx])
        nh = len(rhs_tiles[0])
        pss = [b.take_psum() for _ in range(nh)]
        for hh in range(nh):
            psx = pss[hh]
            for kc in range(n_k):
                s.op("tensor", lambda e, kc=kc, hh=hh, ps=psx, wbx=wbx: e.matmul(
                    ps[:], wbx[:, kc * 128:(kc + 1) * 128], rhs_tiles[kc][hh][:],
                    start=(kc == 0), stop=(extra_group is None and kc == n_k - 1)),
                    reads=[wbx, rhs_tiles[kc][hh]], writes=[psx])
            if extra_group is not None:
                extra_group(oc, hh, psx)
        for hh in range(nh):
            epilogue(oc, hh, pss[hh])



def build_qkvh_launch():
    b = Builder("qkvh")
    nc = b.nc
    d_x = nc.dram_tensor("x", [E, TPC], F32, kind="ExternalInput").ap()
    d_wq = nc.dram_tensor("wqT", [E, E], F32, kind="ExternalInput").ap()
    d_wk = nc.dram_tensor("wkT", [E, 1024], F32, kind="ExternalInput").ap()
    d_wv = nc.dram_tensor("wvT", [E, 1024], F32, kind="ExternalInput").ap()
    d_cos = nc.dram_tensor("cos", [128, TPC], F32, kind="ExternalInput").ap()
    d_sin = nc.dram_tensor("sin", [128, TPC], F32, kind="ExternalInput").ap()
    d_sinsw = nc.dram_tensor("sinsw", [128, TPC], F32, kind="ExternalInput").ap()
    d_hp1 = nc.dram_tensor("hp1", [32, 128, 128], F32, kind="ExternalInput").ap()
    d_hp2 = nc.dram_tensor("hp2", [32, 128, 128], F32, kind="ExternalInput").ap()
    d_hb1 = nc.dram_tensor("hb1", [32, 128], F32, kind="ExternalInput").ap()
    d_hb2 = nc.dram_tensor("hb2", [32, 128], F32, kind="ExternalInput").ap()
    d_qr = nc.dram_tensor("qr", [E, TPC], F32, kind="ExternalOutput").ap()
    d_qh = nc.dram_tensor("qh", [E, TPC], BF16, kind="ExternalOutput").ap()
    d_kr = nc.dram_tensor("kr", [1024, TPC], F32, kind="ExternalOutput").ap()
    d_kh = nc.dram_tensor("kh", [E, TPC], BF16, kind="ExternalOutput").ap()
    d_v = nc.dram_tensor("v", [1024, TPC], BF16, kind="ExternalOutput").ap()

    with b as b:
        s = b.seq
        b.c_ones = ones_consts(b)
        xln = [[b.sbuf([128, 512], BF16, f"xln{kc}_{h}") for h in range(2)] for kc in range(32)]
        cos_t = b.sbuf([128, TPC], F32, "cos_t")
        sin_t = b.sbuf([128, TPC], F32, "sin_t")
        sinsw_t = b.sbuf([128, TPC], F32, "sinsw_t")
        q_f = b.sbufs(2, [128, TPC], F32, "q_f")
        qr_f = b.sbufs(2, [128, TPC], F32, "qr_f")
        qr_b = b.sbufs(2, [128, TPC], BF16, "qr_b")
        h1_f = b.sbufs(2, [128, TPC], F32, "h1_f")
        h1_b = b.sbufs(2, [128, TPC], BF16, "h1_b")
        hh_b = b.sbufs(2, [128, TPC], BF16, "hh_b")
        p1_f = b.sbuf([128, 128], F32, "p1_f")
        p1_b = b.sbufs(2, [128, 128], BF16, "p1_b")
        p2_b = b.sbufs(2, [128, 128], BF16, "p2_b")
        hb1_t = b.sbuf([128, 32], F32, "hb1_t")
        hb2_t = b.sbuf([128, 32], F32, "hb2_t")

        with nc.Block() as block:
            s.op("sync", lambda e: e.dma_start(out=cos_t[:], in_=d_cos), reads=[d_cos],
                 writes=[cos_t], dma=True)
            s.op("sync", lambda e: e.dma_start(out=sin_t[:], in_=d_sin), reads=[d_sin],
                 writes=[sin_t], dma=True)
            s.op("sync", lambda e: e.dma_start(out=sinsw_t[:], in_=d_sinsw), reads=[d_sinsw],
                 writes=[sinsw_t], dma=True)
            for hd in range(32):
                s.op("sync", lambda e, hd=hd: e.dma_start(
                    out=hb1_t[:, hd:hd + 1], in_=d_hb1[hd, :].rearrange("(p o) -> p o", o=1)),
                    reads=[d_hb1], writes=[hb1_t], dma=True)
                s.op("sync", lambda e, hd=hd: e.dma_start(
                    out=hb2_t[:, hd:hd + 1], in_=d_hb2[hd, :].rearrange("(p o) -> p o", o=1)),
                    reads=[d_hb2], writes=[hb2_t], dma=True)

            for h in range(2):
                emit_ln_fm(b, d_x, h * 512, [xln[kc][h] for kc in range(32)])

            def rope(dst_f, src_f, sgn_col):
                lo = slice(0, 64)
                hi = slice(64, 128)
                s.op("vector", lambda e: e.scalar_tensor_tensor(
                    out=dst_f[lo, :], in0=src_f[hi, :], scalar=-1.0, in1=sinsw_t[hi, :],
                    op0=ALU.mult, op1=ALU.mult), reads=[src_f, sinsw_t], writes=[dst_f])
                s.op("vector", lambda e: e.scalar_tensor_tensor(
                    out=dst_f[hi, :], in0=src_f[lo, :], scalar=1.0, in1=sinsw_t[lo, :],
                    op0=ALU.mult, op1=ALU.mult), reads=[src_f, sinsw_t], writes=[dst_f])
                s.op("gpsimd", lambda e: e.tensor_mul(
                    out=src_f[:, :], in0=src_f[:, :], in1=cos_t[:, :]),
                    reads=[src_f, cos_t], writes=[src_f])
                s.op("vector", lambda e: e.tensor_add(
                    out=dst_f[:, :], in0=dst_f[:, :], in1=src_f[:, :]),
                    reads=[dst_f, src_f], writes=[dst_f])

            def emit_hash(head, src_bf, src_f32, out_sign_bf, bufsel):
                p1b = p1_b[bufsel]
                p2b = p2_b[bufsel]
                h1f = h1_f[bufsel]
                h1b = h1_b[bufsel]
                s.op("sync", lambda e, head=head: e.dma_start(
                    out=p1_f[:], in_=d_hp1[head]), reads=[d_hp1], writes=[p1_f], dma=True)
                s.op("vector", lambda e: e.tensor_copy(out=p1b[:], in_=p1_f[:]),
                     reads=[p1_f], writes=[p1b])
                s.op("sync", lambda e, head=head: e.dma_start(
                    out=p1_f[:], in_=d_hp2[head]), reads=[d_hp2], writes=[p1_f], dma=True)
                s.op("vector", lambda e: e.tensor_copy(out=p2b[:], in_=p1_f[:]),
                     reads=[p1_f], writes=[p2b])
                for hh in range(2):
                    ps1 = b.take_psum()
                    s.op("tensor", lambda e, hh=hh, ps1=ps1: e.matmul(
                        ps1[:], p1b[:], src_bf[:, hh * 512:(hh + 1) * 512],
                        start=True, stop=True), reads=[p1b, src_bf], writes=[ps1])
                    s.op("scalar", lambda e, hh=hh, ps1=ps1, head=head: e.activation(
                        out=h1f[:, hh * 512:(hh + 1) * 512], in_=ps1[:], func=AF.Silu,
                        bias=hb1_t[:, head:head + 1]), reads=[ps1, hb1_t], writes=[h1f])
                s.op("vector", lambda e: e.tensor_add(out=h1f[:], in0=h1f[:], in1=src_f32[:]),
                     reads=[h1f, src_f32], writes=[h1f])
                s.op("vector", lambda e: e.tensor_copy(out=h1b[:], in_=h1f[:]),
                     reads=[h1f], writes=[h1b])
                for hh in range(2):
                    ps2 = b.take_psum()
                    s.op("tensor", lambda e, hh=hh, ps2=ps2: e.matmul(
                        ps2[:], p2b[:], h1b[:, hh * 512:(hh + 1) * 512],
                        start=True, stop=True), reads=[p2b, h1b], writes=[ps2])
                    s.op("vector", lambda e, hh=hh, ps2=ps2, head=head: e.scalar_tensor_tensor(
                        out=h1f[:, hh * 512:(hh + 1) * 512], in0=ps2[:],
                        scalar=hb2_t[:, head:head + 1], in1=h1f[:, hh * 512:(hh + 1) * 512],
                        op0=ALU.add, op1=ALU.add), reads=[ps2, hb2_t, h1f], writes=[h1f])
                s.op("scalar", lambda e: e.activation(
                    out=out_sign_bf[:], in_=h1f[:], func=AF.Sign),
                    reads=[h1f], writes=[out_sign_bf])

            def q_epilogue(oc, hh, psx):
                qf = q_f[oc % 2]
                s.op("scalar", lambda e, oc=oc, hh=hh, ps=psx, qf=qf: e.copy(
                    out=qf[:, hh * 512:(hh + 1) * 512], in_=ps[:]),
                    reads=[psx], writes=[qf])
                if hh == 1:
                    qrf = qr_f[oc % 2]
                    qrb = qr_b[oc % 2]
                    sgn = hh_b[oc % 2]
                    rope(qrf, qf, None)
                    s.op("gpsimd", lambda e, qrf=qrf, oc=oc: e.dma_start(
                        out=d_qr[oc * 128:(oc + 1) * 128, :], in_=qrf[:]),
                        reads=[qrf], writes=[d_qr], dma=True)
                    s.op("vector", lambda e, qrb=qrb, qrf=qrf: e.tensor_copy(
                        out=qrb[:], in_=qrf[:]), reads=[qrf], writes=[qrb])
                    emit_hash(oc, qrb, qrf, sgn, oc % 2)
                    s.op("gpsimd", lambda e, sgn=sgn, oc=oc: e.dma_start(
                        out=d_qh[oc * 128:(oc + 1) * 128, :], in_=sgn[:]),
                        reads=[sgn], writes=[d_qh], dma=True)

            emit_proj_stream(b, d_wq, range(32), xln, q_epilogue, wtag="w")

            kr_keep = [b.sbuf([128, TPC], BF16, f"krk{i}") for i in range(8)]

            def k_epilogue(oc, hh, psx):
                qf = q_f[oc % 2]
                s.op("scalar", lambda e, oc=oc, hh=hh, ps=psx, qf=qf: e.copy(
                    out=qf[:, hh * 512:(hh + 1) * 512], in_=ps[:]),
                    reads=[psx], writes=[qf])
                if hh == 1:
                    qrf = qr_f[oc % 2]
                    rope(qrf, qf, None)
                    s.op("gpsimd", lambda e, qrf=qrf, oc=oc: e.dma_start(
                        out=d_kr[oc * 128:(oc + 1) * 128, :], in_=qrf[:]),
                        reads=[qrf], writes=[d_kr], dma=True)
                    s.op("vector", lambda e, oc=oc, qrf=qrf: e.tensor_copy(
                        out=kr_keep[oc][:], in_=qrf[:]), reads=[qrf], writes=[kr_keep[oc]])

            emit_proj_stream(b, d_wk, range(8), xln, k_epilogue, wtag="w")

            krf32 = b.sbufs(1, [128, TPC], F32, "krf32")
            for head in range(32):
                sgn = hh_b[head % 2]
                kf = krf32[0]
                s.op("vector", lambda e, kf=kf, head=head: e.tensor_copy(
                    out=kf[:], in_=kr_keep[head // 4][:]),
                    reads=[kr_keep[head // 4]], writes=[kf])
                emit_hash(head, kr_keep[head // 4], kf, sgn, head % 2)
                s.op("gpsimd", lambda e, sgn=sgn, head=head: e.dma_start(
                    out=d_kh[head * 128:(head + 1) * 128, :], in_=sgn[:]),
                    reads=[sgn], writes=[d_kh], dma=True)

            def v_epilogue(oc, hh, psx):
                vb = h1_b[oc % 2]
                s.op("scalar", lambda e, oc=oc, hh=hh, ps=psx, vb=vb: e.copy(
                    out=vb[:, hh * 512:(hh + 1) * 512], in_=ps[:]),
                    reads=[psx], writes=[vb])
                if hh == 1:
                    s.op("gpsimd", lambda e, vb=vb, oc=oc: e.dma_start(
                        out=d_v[oc * 128:(oc + 1) * 128, :], in_=vb[:]),
                        reads=[vb], writes=[d_v], dma=True)

            emit_proj_stream(b, d_wv, range(8), xln, v_epilogue, wtag="w")

            fw = [s.last_writer[id(d)] for d in (d_qr, d_qh, d_kr, d_kh, d_v)]
            s.finalize(block, final_waits=fw)
    return b.nc


def rope_tables():
    inv_freq = 1.0 / (10000.0 ** (np.arange(0, 128, 2, dtype=np.float32) / 128.0))
    t = np.arange(S, dtype=np.float32)
    freqs = np.outer(t, inv_freq)
    emb = np.concatenate([freqs, freqs], -1)
    return np.cos(emb).astype(np.float32), np.sin(emb).astype(np.float32)


def run_qkvh(x_fm, Wq, Wk, Wv, hp1, hb1, hp2, hb2):
    nc = build_qkvh_launch()
    cos, sin = rope_tables()
    wqT = np.ascontiguousarray(Wq.T)
    wkT = np.ascontiguousarray(Wk.T)
    wvT = np.ascontiguousarray(Wv.T)
    hb1r = np.ascontiguousarray(hb1.reshape(32, 128))
    hb2r = np.ascontiguousarray(hb2.reshape(32, 128))
    in_maps = []
    for c in range(NCORES):
        s0 = (c % 2) * TPC
        in_maps.append({
            "x": np.ascontiguousarray(x_fm[:, c * TPC:(c + 1) * TPC]),
            "wqT": wqT, "wkT": wkT, "wvT": wvT,
            "cos": np.ascontiguousarray(cos[s0:s0 + TPC].T),
            "sin": np.ascontiguousarray(sin[s0:s0 + TPC].T),
            "sinsw": np.ascontiguousarray(np.concatenate(
                [sin[s0:s0 + TPC, 64:], sin[s0:s0 + TPC, :64]], axis=1).T),
            "hp1": hp1, "hp2": hp2, "hb1": hb1r, "hb2": hb2r,
        })
    res = run_bass_kernel_spmd(nc, in_maps, core_ids=list(range(NCORES)))
    cat = lambda k: np.concatenate([res.results[c][k] for c in range(NCORES)], axis=1)
    return {k: cat(k) for k in ("qr", "qh", "kr", "kh", "v")}



NSLOT = 8


def slot_nk(i):
    return 128 * (2 * i + 2)


def build_attn0_launch(debug=False, nheads=32):
    b = Builder("attn0", npsum=7)
    nc = b.nc
    d_qr = nc.dram_tensor("qr", [E, 1024], F32, kind="ExternalInput").ap()
    d_qh = nc.dram_tensor("qh", [E, 1024], BF16, kind="ExternalInput").ap()
    d_kr = nc.dram_tensor("kr", [1024, S], F32, kind="ExternalInput").ap()
    d_kh = nc.dram_tensor("kh", [E, S], BF16, kind="ExternalInput").ap()
    d_v = nc.dram_tensor("v", [1024, S], BF16, kind="ExternalInput").ap()
    d_ks = nc.dram_tensor("ks", [128, NSLOT], F32, kind="ExternalInput").ap()
    d_msk = nc.dram_tensor("msk", [128, NSLOT * 256], BF16, kind="ExternalInput").ap()
    d_ao = nc.dram_tensor("ao", [E, 1024], BF16, kind="ExternalOutput").ap()
    if debug:
        d_dbg_t = nc.dram_tensor("dbg_t", [128, NSLOT], F32, kind="ExternalOutput").ap()
        d_dbg_f = nc.dram_tensor("dbg_f", [128, NSLOT], F32, kind="ExternalOutput").ap()
        d_dbg_attn = nc.dram_tensor("dbg_attn", [128, S], F32, kind="ExternalOutput").ap()

    GW = 2

    with b as b:
        s = b.seq
        b.c_ones = ones_consts(b)
        ps_bf = b.ctx.enter_context(nc.psum_tensor("ps_bf", [128, 1024], BF16))

        identb = b.sbuf([128, 128], BF16, "identb")
        onesb = b.sbuf([128, S], BF16, "onesb")
        ks_t = b.sbuf([128, NSLOT], F32, "ks_t")
        msk_t = b.sbuf([128, NSLOT * 256], BF16, "msk_t")
        kh_t = b.sbuf([128, S], BF16, "kh_t")
        kr_f = b.sbuf([128, S], F32, "kr_f")
        kr_t = b.sbuf([128, S], BF16, "kr_t")
        v_t = b.sbuf([128, S], BF16, "v_t")
        v_tm = b.sbuf([128, S], BF16, "v_tm")
        qr_f = b.sbuf([128, 1024], F32, "qr_f")
        qr_t = b.sbuf([128, 1024], BF16, "qr_t")
        qh_t = b.sbuf([128, 1024], BF16, "qh_t")
        sim_b = [b.sbufs(GW, [128, S], BF16, "sim_b") for _ in range(2)]
        ind_b = [b.sbufs(GW, [128, S], BF16, "ind_b") for _ in range(2)]
        cum_f = [b.sbufs(GW, [128, S], F32, "cum_f") for _ in range(2)]
        msc_f = [b.sbufs(GW, [128, S], F32, "msc_f") for _ in range(2)]
        att_b = [b.sbufs(GW, [128, S], BF16, "att_b") for _ in range(2)]
        atn_b = [b.sbufs(GW, [128, S], BF16, "atn_b") for _ in range(2)]
        atT_b = [b.sbufs(GW, [128, S], BF16, "atT_b") for _ in range(2)]
        st_lo = [[b.sbuf([128, GW], F32, f"st_lo{i}_{p}") for p in range(2)] for i in range(2)]
        st_hi = [[b.sbuf([128, GW], F32, f"st_hi{i}_{p}") for p in range(2)] for i in range(2)]
        st_mid = [b.sbuf([128, GW], F32, f"st_mid{i}") for i in range(2)]
        st_cnt = [b.sbuf([128, GW], F32, f"st_cnt{i}") for i in range(2)]
        st_cnth = [[b.sbuf([128, GW], F32, f"st_cnth{i}_{p}") for p in range(2)] for i in range(2)]
        st_sel = [b.sbuf([128, GW], U8, f"st_sel{i}") for i in range(2)]
        st_t2 = [b.sbuf([128, GW], F32, f"st_t2{i}") for i in range(2)]
        st_f = [b.sbuf([128, GW], F32, f"st_f{i}") for i in range(2)]
        nmx = [b.sbuf([128, GW], F32, f"nmx{i}") for i in range(2)]
        ebias = [b.sbuf([128, GW], F32, f"ebias{i}") for i in range(2)]
        sume = [b.sbuf([128, GW], F32, f"sume{i}") for i in range(2)]
        rec = [b.sbuf([128, GW], F32, f"rec{i}") for i in range(2)]
        out_ev = b.sbufs(2, [128, 128], BF16, "out_ev")

        SCL = 1.0 / math.sqrt(128.0)

        with nc.Block() as block:
            s.op("gpsimd", lambda e: e.memset(identb[:], 1.0), writes=[identb])
            s.op("gpsimd", lambda e: e.affine_select(
                out=identb[:], in_=identb[:], compare_op=ALU.is_equal, fill=0.0,
                base=0, pattern=[[-1, 128]], channel_multiplier=1),
                reads=[identb], writes=[identb])
            s.op("gpsimd", lambda e: e.memset(onesb[:], 1.0), writes=[onesb])
            s.op("sync", lambda e: e.dma_start(out=ks_t[:], in_=d_ks), reads=[d_ks],
                 writes=[ks_t], dma=True)
            s.op("sync", lambda e: e.dma_start(out=msk_t[:], in_=d_msk), reads=[d_msk],
                 writes=[msk_t], dma=True)

            for h in range(nheads):
                kv = h // 4
                if h % 4 == 0:
                    s.op("sync", lambda e, kv=kv: e.dma_start(
                        out=kr_f[:], in_=d_kr[kv * 128:(kv + 1) * 128, :]),
                        reads=[d_kr], writes=[kr_f], dma=True)
                    s.op("vector", lambda e: e.tensor_copy(out=kr_t[:], in_=kr_f[:]),
                         reads=[kr_f], writes=[kr_t])
                    s.op("sync", lambda e, kv=kv: e.dma_start(
                        out=v_t[:], in_=d_v[kv * 128:(kv + 1) * 128, :]),
                        reads=[d_v], writes=[v_t], dma=True)
                    for g2 in range(2):
                        for j in range(8):
                            kc = g2 * 8 + j
                            s.op("tensor", lambda e, kc=kc, j=j: e.transpose(
                                ps_bf[:, j * 128:(j + 1) * 128],
                                v_t[:, kc * 128:(kc + 1) * 128], identb[:]),
                                reads=[v_t, identb], writes=[ps_bf])
                        s.op("vector", lambda e, g2=g2: e.tensor_copy(
                            out=v_tm[:, g2 * 1024:(g2 + 1) * 1024], in_=ps_bf[:]),
                            reads=[ps_bf], writes=[v_tm])
                s.op("sync", lambda e, h=h: e.dma_start(
                    out=kh_t[:], in_=d_kh[h * 128:(h + 1) * 128, :]),
                    reads=[d_kh], writes=[kh_t], dma=True)
                s.op("sync", lambda e, h=h: e.dma_start(
                    out=qr_f[:], in_=d_qr[h * 128:(h + 1) * 128, :]),
                    reads=[d_qr], writes=[qr_f], dma=True)
                s.op("vector", lambda e: e.tensor_copy(out=qr_t[:], in_=qr_f[:]),
                     reads=[qr_f], writes=[qr_t])
                s.op("sync", lambda e, h=h: e.dma_start(
                    out=qh_t[:], in_=d_qh[h * 128:(h + 1) * 128, :]),
                    reads=[d_qh], writes=[qh_t], dma=True)

                for grp in range(NSLOT // GW):
                    pb = grp % 2
                    simt, indt, cumt, msct = sim_b[pb], ind_b[pb], cum_f[pb], msc_f[pb]
                    attt, atnt, atTt = att_b[pb], atn_b[pb], atT_b[pb]
                    lo2, hi2, mid = st_lo[pb], st_hi[pb], st_mid[pb]
                    cnt, cnth2, selu = st_cnt[pb], st_cnth[pb], st_sel[pb]
                    t2v, fv = st_t2[pb], st_f[pb]
                    nm, eb, se, rc = nmx[pb], ebias[pb], sume[pb], rec[pb]
                    slots = [grp * GW + j for j in range(GW)]

                    for j, sl in enumerate(slots):
                        nk = slot_nk(sl)
                        for ck in range(ceil_div(nk, 512)):
                            w = min(512, nk - ck * 512)
                            psx = b.take_psum()
                            s.op("tensor", lambda e, sl=sl, ck=ck, w=w, ps=psx: e.matmul(
                                ps[:, :w], qh_t[:, sl * 128:(sl + 1) * 128],
                                kh_t[:, ck * 512:ck * 512 + w], start=True, stop=True),
                                reads=[qh_t, kh_t], writes=[psx])
                            s.op("scalar", lambda e, j=j, ck=ck, w=w, ps=psx, simt=simt: e.copy(
                                out=simt[j][:, ck * 512:ck * 512 + w], in_=ps[:, :w]),
                                reads=[psx], writes=[simt[j]])
                        s.op("vector", lambda e, j=j, sl=sl, nk=nk, simt=simt: e.tensor_add(
                            out=simt[j][:, nk - 256:nk], in0=simt[j][:, nk - 256:nk],
                            in1=msk_t[:, sl * 256:(sl + 1) * 256]),
                            reads=[simt[j], msk_t], writes=[simt[j]])

                    s.op("gpsimd", lambda e, lo=lo2[0]: e.memset(lo[:], -128.0), writes=[lo2[0]])
                    s.op("gpsimd", lambda e, hi=hi2[0]: e.memset(hi[:], 384.0), writes=[hi2[0]])
                    s.op("gpsimd", lambda e, mid=mid: e.memset(mid[:], 128.0), writes=[mid])
                    s.op("gpsimd", lambda e, ch=cnth2[0]: e.memset(ch[:], 0.0), writes=[cnth2[0]])
                    for rnd in range(8):
                        cu, nx = rnd % 2, (rnd + 1) % 2
                        lo_c, lo_n = lo2[cu], lo2[nx]
                        hi_c, hi_n = hi2[cu], hi2[nx]
                        ch_c, ch_n = cnth2[cu], cnth2[nx]
                        for j, sl in enumerate(slots):
                            nk = slot_nk(sl)
                            s.op("vector", lambda e, j=j, nk=nk, simt=simt, indt=indt,
                                 mid=mid, cnt=cnt: e.tensor_scalar(
                                 out=indt[j][:, :nk], in0=simt[j][:, :nk],
                                 scalar1=mid[:, j:j + 1], scalar2=0.0,
                                 op0=ALU.is_ge, op1=ALU.add, accum_out=cnt[:, j:j + 1]),
                                 reads=[simt[j], mid], writes=[indt[j], cnt])
                        s.op("vector", lambda e, cnt=cnt, selu=selu, grp=grp: e.tensor_tensor(
                            out=selu[:], in0=cnt[:], in1=ks_t[:, grp * GW:(grp + 1) * GW],
                            op=ALU.is_ge), reads=[cnt, ks_t], writes=[selu])
                        s.op("vector", lambda e, selu=selu, mid=mid, a=lo_c, o=lo_n: e.select(
                            out=o[:], mask=selu[:], on_true=mid[:], on_false=a[:], add_drain=True),
                            reads=[selu, mid, lo_c], writes=[lo_n])
                        s.op("vector", lambda e, selu=selu, mid=mid, a=hi_c, o=hi_n: e.select(
                            out=o[:], mask=selu[:], on_true=a[:], on_false=mid[:], add_drain=True),
                            reads=[selu, hi_c, mid], writes=[hi_n])
                        s.op("vector", lambda e, selu=selu, cnt=cnt, a=ch_c, o=ch_n: e.select(
                            out=o[:], mask=selu[:], on_true=a[:], on_false=cnt[:], add_drain=True),
                            reads=[selu, ch_c, cnt], writes=[ch_n])
                        if rnd < 7:
                            s.op("vector", lambda e, mid=mid, a=lo_n, c=hi_n: e.tensor_add(
                                out=mid[:], in0=a[:], in1=c[:]), reads=[lo_n, hi_n], writes=[mid])
                            s.op("vector", lambda e, mid=mid: e.tensor_scalar(
                                out=mid[:], in0=mid[:], scalar1=0.5, scalar2=None,
                                op0=ALU.mult), reads=[mid], writes=[mid])
                    lo = lo2[0]
                    cnth = cnth2[0]
                    s.op("vector", lambda e, t2v=t2v, lo=lo: e.tensor_scalar(
                        out=t2v[:], in0=lo[:], scalar1=2.0, scalar2=None, op0=ALU.add),
                        reads=[lo], writes=[t2v])
                    s.op("vector", lambda e, fv=fv, cnth=cnth, grp=grp: e.tensor_tensor(
                        out=fv[:], in0=ks_t[:, grp * GW:(grp + 1) * GW], in1=cnth[:],
                        op=ALU.subtract), reads=[ks_t, cnth], writes=[fv])

                    for j, sl in enumerate(slots):
                        nk = slot_nk(sl)
                        s.op("vector", lambda e, j=j, nk=nk, simt=simt, indt=indt, lo=lo:
                             e.tensor_scalar(out=indt[j][:, :nk], in0=simt[j][:, :nk],
                             scalar1=lo[:, j:j + 1], scalar2=None, op0=ALU.is_equal),
                             reads=[simt[j], lo], writes=[indt[j]])
                        s.op("vector", lambda e, j=j, nk=nk, indt=indt, cumt=cumt:
                             e.tensor_tensor_scan(out=cumt[j][:, :nk], data0=onesb[:, :nk],
                             data1=indt[j][:, :nk], initial=0.0, op0=ALU.mult, op1=ALU.add),
                             reads=[indt[j], onesb], writes=[cumt[j]])
                        s.op("gpsimd", lambda e, j=j, nk=nk, cumt=cumt, indt=indt, fv=fv:
                             e.tensor_scalar(out=indt[j][:, :nk], in0=cumt[j][:, :nk],
                             scalar1=fv[:, j:j + 1], scalar2=2.0, op0=ALU.is_le,
                             op1=ALU.mult), reads=[cumt[j], fv], writes=[indt[j]])
                        s.op("vector", lambda e, j=j, nk=nk, simt=simt, indt=indt:
                             e.tensor_add(out=simt[j][:, :nk], in0=simt[j][:, :nk],
                             in1=indt[j][:, :nk]), reads=[simt[j], indt[j]], writes=[simt[j]])
                        s.op("vector", lambda e, j=j, nk=nk, simt=simt, indt=indt, t2v=t2v:
                             e.tensor_scalar(out=indt[j][:, :nk], in0=simt[j][:, :nk],
                             scalar1=t2v[:, j:j + 1], scalar2=1.0, op0=ALU.is_ge,
                             op1=ALU.subtract), reads=[simt[j], t2v], writes=[indt[j]])
                        for ck in range(ceil_div(nk, 512)):
                            w = min(512, nk - ck * 512)
                            psx = b.take_psum()
                            s.op("tensor", lambda e, sl=sl, ck=ck, w=w, ps=psx: e.matmul(
                                ps[:, :w], qr_t[:, sl * 128:(sl + 1) * 128],
                                kr_t[:, ck * 512:ck * 512 + w], start=True, stop=True),
                                reads=[qr_t, kr_t], writes=[psx])
                            s.op("vector", lambda e, j=j, ck=ck, w=w, ps=psx, indt=indt,
                                 msct=msct: e.scalar_tensor_tensor(
                                 out=msct[j][:, ck * 512:ck * 512 + w],
                                 in0=indt[j][:, ck * 512:ck * 512 + w], scalar=1e30,
                                 in1=ps[:, :w], op0=ALU.mult, op1=ALU.add),
                                 reads=[indt[j], psx], writes=[msct[j]])
                        s.op("vector", lambda e, j=j, nk=nk, msct=msct, nm=nm: e.reduce_max(
                            out=nm[:, j:j + 1], in_=msct[j][:, :nk], axis=AX.X, negate=True),
                            reads=[msct[j]], writes=[nm])
                        s.op("vector", lambda e, j=j, nm=nm, eb=eb: e.tensor_scalar(
                            out=eb[:, j:j + 1], in0=nm[:, j:j + 1], scalar1=SCL,
                            scalar2=None, op0=ALU.mult), reads=[nm], writes=[eb])
                        s.op("scalar", lambda e, j=j, nk=nk, msct=msct, attt=attt, eb=eb,
                             se=se: e.activation(
                             out=attt[j][:, :nk], in_=msct[j][:, :nk], func=AF.Exp,
                             scale=SCL, bias=eb[:, j:j + 1], accum_out=se[:, j:j + 1]),
                             reads=[msct[j], eb], writes=[attt[j], se])
                        s.op("vector", lambda e, j=j, se=se, rc=rc: e.reciprocal(
                            out=rc[:, j:j + 1], in_=se[:, j:j + 1]), reads=[se], writes=[rc])
                        s.op("vector", lambda e, j=j, nk=nk, attt=attt, atnt=atnt, rc=rc:
                             e.tensor_scalar(out=atnt[j][:, :nk], in0=attt[j][:, :nk],
                             scalar1=rc[:, j:j + 1], scalar2=None, op0=ALU.mult),
                             reads=[attt[j], rc], writes=[atnt[j]])
                        nch = nk // 128
                        for g2 in range(ceil_div(nch, 8)):
                            jj0 = g2 * 8
                            jj1 = min(jj0 + 8, nch)
                            for jj in range(jj0, jj1):
                                s.op("tensor", lambda e, j=j, jj=jj, atnt=atnt: e.transpose(
                                    ps_bf[:, (jj % 8) * 128:(jj % 8 + 1) * 128],
                                    atnt[j][:, jj * 128:(jj + 1) * 128], identb[:]),
                                    reads=[atnt[j], identb], writes=[ps_bf])
                            s.op("vector", lambda e, j=j, jj0=jj0, jj1=jj1, atTt=atTt:
                                 e.tensor_copy(
                                 out=atTt[j][:, jj0 * 128:jj1 * 128],
                                 in_=ps_bf[:, :(jj1 - jj0) * 128]),
                                 reads=[ps_bf], writes=[atTt[j]])
                        psx = b.take_psum()
                        for kc in range(nch):
                            s.op("tensor", lambda e, j=j, kc=kc, nch=nch, ps=psx,
                                 atTt=atTt: e.matmul(
                                 ps[:, :128], v_tm[:, kc * 128:(kc + 1) * 128],
                                 atTt[j][:, kc * 128:(kc + 1) * 128],
                                 start=(kc == 0), stop=(kc == nch - 1)),
                                 reads=[v_tm, atTt[j]], writes=[psx])
                        oev = out_ev[(grp * GW + j) % 2]
                        s.op("scalar", lambda e, ps=psx, oev=oev: e.copy(
                            out=oev[:], in_=ps[:, :128]), reads=[psx], writes=[oev])
                        s.op("gpsimd", lambda e, h=h, sl=sl, oev=oev: e.dma_start(
                            out=d_ao[h * 128:(h + 1) * 128, sl * 128:(sl + 1) * 128],
                            in_=oev[:]), reads=[oev], writes=[d_ao], dma=True)
                        if debug and h == 0:
                            s.op("gpsimd", lambda e, j=j, sl=sl, lo=lo: e.dma_start(
                                out=d_dbg_t[:, sl:sl + 1], in_=lo[:, j:j + 1],
                                allow_slow_non_contiguous=True),
                                reads=[lo], writes=[d_dbg_t], dma=True)
                            s.op("gpsimd", lambda e, j=j, sl=sl, fv=fv: e.dma_start(
                                out=d_dbg_f[:, sl:sl + 1], in_=fv[:, j:j + 1],
                                allow_slow_non_contiguous=True),
                                reads=[fv], writes=[d_dbg_f], dma=True)
                            if sl == NSLOT - 1:
                                dbg_at = b.sbuf([128, S], F32, "dbg_at")
                                s.op("vector", lambda e, j=j, atnt=atnt: e.tensor_copy(
                                    out=dbg_at[:], in_=atnt[j][:]),
                                    reads=[atnt[j]], writes=[dbg_at])
                                s.op("gpsimd", lambda e: e.dma_start(
                                    out=d_dbg_attn, in_=dbg_at[:]),
                                    reads=[dbg_at], writes=[d_dbg_attn], dma=True)

            s.finalize(block, final_waits=[s.last_writer[id(d_ao)]])
    return b.nc


def attn0_host_inputs(core, qr, qh, kr, kh, v):
    bb = core // 2
    par = core % 2
    tiles = [2 * i + par for i in range(NSLOT)]
    tok0 = bb * S
    cols = np.concatenate([np.arange(tok0 + t * 128, tok0 + (t + 1) * 128) for t in tiles])
    ks = np.zeros((128, NSLOT), np.float32)
    msk = np.zeros((NSLOT, 128, 256), np.float32)
    for i, t in enumerate(tiles):
        svec = t * 128 + np.arange(128)
        ks[:, i] = np.minimum(410, svec + 1)
        nk = slot_nk(i)
        jj = np.arange(nk - 256, nk)
        msk[i] = np.where(jj[None, :] <= svec[:, None], 0.0, -2048.0)
    return {
        "qr": np.ascontiguousarray(qr[:, cols]),
        "qh": np.ascontiguousarray(qh[:, cols]),
        "kr": np.ascontiguousarray(kr[:, tok0:tok0 + S]),
        "kh": np.ascontiguousarray(kh[:, tok0:tok0 + S]),
        "v": np.ascontiguousarray(v[:, tok0:tok0 + S]),
        "ks": ks,
        "msk": np.ascontiguousarray(msk.transpose(1, 0, 2).reshape(128, NSLOT * 256)).astype(BF),
    }, cols


def run_attn0(qr, qh, kr, kh, v, debug=False):
    nc = build_attn0_launch(debug=debug)
    in_maps = []
    colss = []
    for c in range(NCORES):
        m, cols = attn0_host_inputs(c, qr, qh, kr, kh, v)
        in_maps.append(m)
        colss.append(cols)
    res = run_bass_kernel_spmd(nc, in_maps, core_ids=list(range(NCORES)))
    ao = np.zeros((E, TOK), dtype=BF)
    for c in range(NCORES):
        ao[:, colss[c]] = res.results[c]["ao"]
    if debug:
        return ao, res.results
    return ao



def build_mfront_launch():
    b = Builder("mfront")
    nc = b.nc
    d_x = nc.dram_tensor("x", [E, TPC], F32, kind="ExternalInput").ap()
    d_w = nc.dram_tensor("wT", [E, 81 * 128], F32, kind="ExternalInput").ap()
    d_dtb = nc.dram_tensor("dtb", [64, 1], F32, kind="ExternalInput").ap()
    d_gs = nc.dram_tensor("gs", [E, TPC], BF16, kind="ExternalOutput").ap()
    d_hbc = nc.dram_tensor("hbc", [6144, TPC], F32, kind="ExternalOutput").ap()
    d_dtp = nc.dram_tensor("dtp", [64, TPC], F32, kind="ExternalOutput").ap()

    with b as b:
        s = b.seq
        b.c_ones = ones_consts(b)
        xln = [[b.sbuf([128, 512], BF16, f"xln{kc}_{h}") for h in range(2)] for kc in range(32)]
        gs_t = b.sbufs(2, [128, TPC], BF16, "gs_t")
        hb_t = b.sbufs(2, [128, TPC], F32, "hb_t")
        dt_t = b.sbuf([128, TPC], F32, "dt_t")
        dtb_t = b.sbuf([64, 1], F32, "dtb_t")
        sp1 = b.sbuf([64, TPC], F32, "sp1")
        sp2 = b.sbuf([64, TPC], F32, "sp2")

        with nc.Block() as block:
            s.op("sync", lambda e: e.dma_start(out=dtb_t[:], in_=d_dtb), reads=[d_dtb],
                 writes=[dtb_t], dma=True)
            for h in range(2):
                emit_ln_fm(b, d_x, h * 512, [xln[kc][h] for kc in range(32)])

            def epi(oc, hh, psx):
                if oc < 32:
                    gt = gs_t[oc % 2]
                    s.op("scalar", lambda e, hh=hh, ps=psx, gt=gt: e.activation(
                        out=gt[:, hh * 512:(hh + 1) * 512], in_=ps[:], func=AF.Silu),
                        reads=[psx], writes=[gt])
                    if hh == 1:
                        s.op("gpsimd", lambda e, gt=gt, oc=oc: e.dma_start(
                            out=d_gs[oc * 128:(oc + 1) * 128, :], in_=gt[:]),
                            reads=[gt], writes=[d_gs], dma=True)
                elif oc < 80:
                    ht = hb_t[oc % 2]
                    s.op("scalar", lambda e, hh=hh, ps=psx, ht=ht: e.copy(
                        out=ht[:, hh * 512:(hh + 1) * 512], in_=ps[:]),
                        reads=[psx], writes=[ht])
                    if hh == 1:
                        s.op("gpsimd", lambda e, ht=ht, oc=oc: e.dma_start(
                            out=d_hbc[(oc - 32) * 128:(oc - 31) * 128, :], in_=ht[:]),
                            reads=[ht], writes=[d_hbc], dma=True)
                else:
                    s.op("scalar", lambda e, hh=hh, ps=psx: e.copy(
                        out=dt_t[:, hh * 512:(hh + 1) * 512], in_=ps[:]),
                        reads=[psx], writes=[dt_t])
                    if hh == 1:
                        s.op("vector", lambda e: e.tensor_scalar(
                            out=dt_t[0:64, :], in0=dt_t[0:64, :], scalar1=dtb_t[:],
                            scalar2=None, op0=ALU.add), reads=[dt_t, dtb_t], writes=[dt_t])
                        s.op("scalar", lambda e: e.activation(
                            out=sp1[:], in_=dt_t[0:64, :], func=AF.Abs),
                            reads=[dt_t], writes=[sp1])
                        s.op("scalar", lambda e: e.activation(
                            out=sp1[:], in_=sp1[:], func=AF.Exp, scale=-1.0),
                            reads=[sp1], writes=[sp1])
                        one_t = b.const_tile(1.0, parts=64)
                        s.op("scalar", lambda e: e.activation(
                            out=sp1[:], in_=sp1[:], func=AF.Ln, bias=one_t[:]),
                            reads=[sp1, one_t], writes=[sp1])
                        s.op("vector", lambda e: e.tensor_scalar(
                            out=sp2[:], in0=dt_t[0:64, :], scalar1=0.0, scalar2=None,
                            op0=ALU.max), reads=[dt_t], writes=[sp2])
                        s.op("vector", lambda e: e.tensor_add(
                            out=sp1[:], in0=sp1[:], in1=sp2[:]), reads=[sp1, sp2], writes=[sp1])
                        s.op("gpsimd", lambda e: e.dma_start(out=d_dtp, in_=sp1[:]),
                             reads=[sp1], writes=[d_dtp], dma=True)

            emit_proj_stream(b, d_w, range(81), xln, epi, wtag="w")
            s.finalize(block, final_waits=[s.last_writer[id(d_gs)],
                                           s.last_writer[id(d_hbc)],
                                           s.last_writer[id(d_dtp)]])
    return b.nc


def run_mfront(x_fm, in_proj_w, dt_bias):
    nc = build_mfront_launch()
    wT = np.zeros((E, 81 * 128), np.float32)
    wT[:, :10304] = in_proj_w.T
    in_maps = []
    for c in range(NCORES):
        in_maps.append({
            "x": np.ascontiguousarray(x_fm[:, c * TPC:(c + 1) * TPC]),
            "wT": wT,
            "dtb": np.ascontiguousarray(dt_bias.reshape(64, 1)),
        })
    res = run_bass_kernel_spmd(nc, in_maps, core_ids=list(range(NCORES)))
    cat = lambda k: np.concatenate([res.results[c][k] for c in range(NCORES)], axis=1)
    return {k: cat(k) for k in ("gs", "hbc", "dtp")}



def build_ssd_launch(nj=16):
    b = Builder("ssd", npsum=8)
    nc = b.nc
    d_hx = nc.dram_tensor("hx", [2048, S + 3], F32, kind="ExternalInput").ap()
    d_bc = nc.dram_tensor("bc", [1024, S + 3], F32, kind="ExternalInput").ap()
    d_cwx = nc.dram_tensor("cwx", [2048, 4], F32, kind="ExternalInput").ap()
    d_cbx = nc.dram_tensor("cbx", [2048, 1], F32, kind="ExternalInput").ap()
    d_cwb = nc.dram_tensor("cwb", [1024, 4], F32, kind="ExternalInput").ap()
    d_cbb = nc.dram_tensor("cbb", [1024, 1], F32, kind="ExternalInput").ap()
    d_dtp = nc.dram_tensor("dtp", [32, S], F32, kind="ExternalInput").ap()
    d_alog = nc.dram_tensor("alog", [32, 1], F32, kind="ExternalInput").ap()
    d_dbc = nc.dram_tensor("dbc", [128, 32], F32, kind="ExternalInput").ap()
    d_y = nc.dram_tensor("y", [S, 2048], F32, kind="ExternalOutput").ap()

    NCH = 16

    with b as b:
        s = b.seq
        b.c_ones = ones_consts(b)
        o_col, o_row, _, _ = b.c_ones
        identf = b.sbuf([128, 128], F32, "identf")
        trimask = b.sbuf([128, 128], F32, "trimask")
        ones32 = b.sbuf([32, S], F32, "ones32")
        dtp_t = b.sbuf([32, S], F32, "dtp_t")
        alog_t = b.sbuf([32, 1], F32, "alog_t")
        a_t = b.sbuf([32, 1], F32, "a_t")
        dbc_t = b.sbuf([128, 32], F32, "dbc_t")
        cdt_t = b.sbuf([32, 128], F32, "cdt_t")
        l_t = b.sbuf([32, 128], F32, "l_t")
        gam_t = b.sbuf([32, 128], F32, "gam_t")
        w_t = b.sbuf([32, 128], F32, "w_t")
        gend = b.sbuf([32, 16], F32, "gend")
        gend_row = b.sbuf([1, 512], F32, "gend_row")
        gebc = b.sbuf([128, 512], F32, "gebc")
        TT = [b.sbuf([128, 128], F32, f"TT{c}") for c in range(NCH)]
        abc = b.sbuf([128, 32], F32, "abc")
        a_row = b.sbuf([1, 32], F32, "a_row")
        hx_raw = b.sbufs(2, [128, S + 3], F32, "hx_raw")
        acc_c = b.sbufs(2, [128, S], F32, "acc_c")
        xsil = b.sbufs(2, [128, S], F32, "xsil")
        cw_t = b.sbuf([128, 4], F32, "cw_t")
        cb_t = b.sbuf([128, 1], F32, "cb_t")
        Bc = [b.sbuf([128, S], F32, f"Bc{g}") for g in range(4)]
        Cc = [b.sbuf([128, S], F32, f"Cc{g}") for g in range(4)]
        St = [b.sbuf([128, 64], F32, f"St{h}") for h in range(32)]
        x_tm = b.sbufs(2, [128, 64], F32, "x_tm")
        dtx_tm = b.sbufs(2, [128, 64], F32, "dtx_tm")
        lrow = b.sbufs(2, [1, 128], F32, "lrow")
        dpr = b.sbufs(2, [128, 128], F32, "dpr")
        epr = b.sbufs(2, [128, 128], F32, "epr")
        wpr = b.sbufs(2, [128, 128], F32, "wpr")
        bt_tm = b.sbufs(2, [128, 128], F32, "bt_tm")
        bw_t = b.sbufs(2, [128, 128], F32, "bw_t")
        ytmp = b.sbufs(2, [128, 64], F32, "ytmp")
        yout = b.sbufs(2, [128, 64], F32, "yout")

        with nc.Block() as block:
            s.op("gpsimd", lambda e: e.memset(identf[:], 1.0), writes=[identf])
            s.op("gpsimd", lambda e: e.affine_select(
                out=identf[:], in_=identf[:], compare_op=ALU.is_equal, fill=0.0,
                base=0, pattern=[[-1, 128]], channel_multiplier=1),
                reads=[identf], writes=[identf])
            s.op("gpsimd", lambda e: e.memset(trimask[:], 0.0), writes=[trimask])
            s.op("gpsimd", lambda e: e.affine_select(
                out=trimask[:], in_=trimask[:], compare_op=ALU.is_ge, fill=-100000.0,
                base=0, pattern=[[1, 128]], channel_multiplier=-1),
                reads=[trimask], writes=[trimask])
            s.op("gpsimd", lambda e: e.memset(ones32[:], 1.0), writes=[ones32])
            s.op("sync", lambda e: e.dma_start(out=dtp_t[:], in_=d_dtp), reads=[d_dtp],
                 writes=[dtp_t], dma=True)
            s.op("sync", lambda e: e.dma_start(out=alog_t[:], in_=d_alog), reads=[d_alog],
                 writes=[alog_t], dma=True)
            s.op("sync", lambda e: e.dma_start(out=dbc_t[:], in_=d_dbc), reads=[d_dbc],
                 writes=[dbc_t], dma=True)
            s.op("scalar", lambda e: e.activation(out=a_t[:], in_=alog_t[:], func=AF.Exp),
                 reads=[alog_t], writes=[a_t])
            s.op("vector", lambda e: e.tensor_scalar(
                out=a_t[:], in0=a_t[:], scalar1=-1.0, scalar2=None, op0=ALU.mult),
                reads=[a_t], writes=[a_t])
            s.op("sync", lambda e: e.dma_start(out=a_row[:], in_=a_t[:]),
                 reads=[a_t], writes=[a_row], dma=True)
            ps1 = b.take_psum()
            s.op("tensor", lambda e, ps=ps1: e.matmul(ps[:, :32], o_row[:], a_row[:],
                                                      start=True, stop=True),
                 reads=[o_row, a_row], writes=[ps1])
            s.op("vector", lambda e, ps=ps1: e.tensor_copy(out=abc[:], in_=ps[:, :32]),
                 reads=[ps1], writes=[abc])

            for c in range(NCH):
                s.op("vector", lambda e, c=c: e.tensor_tensor_scan(
                    out=cdt_t[:], data0=ones32[:, :128], data1=dtp_t[:, c * 128:(c + 1) * 128],
                    initial=0.0, op0=ALU.mult, op1=ALU.add),
                    reads=[ones32, dtp_t], writes=[cdt_t])
                s.op("vector", lambda e: e.tensor_scalar(
                    out=l_t[:], in0=cdt_t[:], scalar1=a_t[:], scalar2=None, op0=ALU.mult),
                    reads=[cdt_t, a_t], writes=[l_t])
                s.op("scalar", lambda e: e.activation(out=gam_t[:], in_=l_t[:], func=AF.Exp),
                     reads=[l_t], writes=[gam_t])
                s.op("scalar", lambda e: e.activation(
                    out=w_t[:], in_=l_t[:], func=AF.Exp, scale=-1.0,
                    bias=l_t[:, 127:128]), reads=[l_t], writes=[w_t])
                s.op("vector", lambda e, c=c: e.tensor_copy(
                    out=gend[:, c:c + 1], in_=gam_t[:, 127:128]),
                    reads=[gam_t], writes=[gend])
                psx = b.take_psum()
                for k, tile in enumerate((l_t, gam_t, w_t)):
                    s.op("tensor", lambda e, k=k, tile=tile, ps=psx: e.transpose(
                        ps[:, k * 32:(k + 1) * 32], tile[:], identf[:32, :32]),
                        reads=[tile, identf], writes=[psx])
                s.op("tensor", lambda e, c=c, ps=psx: e.transpose(
                    ps[:, 96:128], dtp_t[:, c * 128:(c + 1) * 128], identf[:32, :32]),
                    reads=[dtp_t, identf], writes=[psx])
                s.op("vector", lambda e, c=c, ps=psx: e.tensor_copy(out=TT[c][:], in_=ps[:, :128]),
                     reads=[psx], writes=[TT[c]])
            s.op("sync", lambda e: e.dma_start(out=gend_row[:], in_=gend[:]), reads=[gend],
                 writes=[gend_row], dma=True)
            ps2 = b.take_psum()
            s.op("tensor", lambda e, ps=ps2: e.matmul(ps[:], o_row[:], gend_row[:],
                                                      start=True, stop=True),
                 reads=[o_row, gend_row], writes=[ps2])
            s.op("vector", lambda e, ps=ps2: e.tensor_copy(out=gebc[:], in_=ps[:]),
                 reads=[ps2], writes=[gebc])

            for h in range(32):
                s.op("gpsimd", lambda e, h=h: e.memset(St[h][:], 0.0), writes=[St[h]])
            def emit_conv(dst, d_src, row0, d_cw, d_cb, cwrow0, buf_idx):
                raw = hx_raw[buf_idx]
                acc = acc_c[buf_idx]
                s.op("sync", lambda e: e.dma_start(
                    out=raw[:], in_=d_src[row0:row0 + 128, :]), reads=[d_src],
                    writes=[raw], dma=True)
                s.op("sync", lambda e: e.dma_start(
                    out=cw_t[:], in_=d_cw[cwrow0:cwrow0 + 128, :]), reads=[d_cw],
                    writes=[cw_t], dma=True)
                s.op("sync", lambda e: e.dma_start(
                    out=cb_t[:], in_=d_cb[cwrow0:cwrow0 + 128, :]), reads=[d_cb],
                    writes=[cb_t], dma=True)
                s.op("vector", lambda e: e.tensor_scalar(
                    out=acc[:], in0=raw[:, 0:S], scalar1=cw_t[:, 0:1], scalar2=None,
                    op0=ALU.mult), reads=[raw, cw_t], writes=[acc])
                for k in range(1, 4):
                    s.op("vector", lambda e, k=k: e.scalar_tensor_tensor(
                        out=acc[:], in0=raw[:, k:S + k], scalar=cw_t[:, k:k + 1],
                        in1=acc[:], op0=ALU.mult, op1=ALU.add),
                        reads=[raw, cw_t, acc], writes=[acc])
                s.op("scalar", lambda e: e.activation(
                    out=dst[:], in_=acc[:], func=AF.Silu, bias=cb_t[:]),
                    reads=[acc, cb_t], writes=[dst])

            for g in range(4):
                emit_conv(Bc[g], d_bc, g * 128, d_cwb, d_cbb, g * 128, g % 2)
            for g in range(4):
                emit_conv(Cc[g], d_bc, 512 + g * 128, d_cwb, d_cbb, 512 + g * 128, g % 2)

            for j in range(nj):
                xs = xsil[j % 2]
                emit_conv(xs, d_hx, j * 128, d_cwx, d_cbx, j * 128, j % 2)
                for hh2 in range(2):
                    h = 2 * j + hh2
                    g = h // 8
                    pbase = 64 * hh2
                    for tc in range(NCH):
                        u = (h * NCH + tc) % 2
                        xt, dxt = x_tm[u], dtx_tm[u]
                        lr, dp, ep2, wp = lrow[u], dpr[u], epr[u], wpr[u]
                        btm, bwt = bt_tm[u], bw_t[u]
                        yt, yo = ytmp[u], yout[u]
                        psa = b.take_psum()
                        s.op("tensor", lambda e, xs=xs, tc=tc, pbase=pbase, ps=psa: e.transpose(
                            ps[:, :64], xs[pbase:pbase + 64, tc * 128:(tc + 1) * 128],
                            identf[pbase:pbase + 64, pbase:pbase + 64]),
                            reads=[xs, identf], writes=[psa])
                        s.op("vector", lambda e, xt=xt, ps=psa: e.tensor_copy(
                            out=xt[:], in_=ps[:, :64]), reads=[psa], writes=[xt])
                        s.op("vector", lambda e, dxt=dxt, xt=xt, tc=tc, h=h: e.tensor_scalar(
                            out=dxt[:], in0=xt[:], scalar1=TT[tc][:, 96 + h:97 + h],
                            scalar2=None, op0=ALU.mult), reads=[xt, TT[tc]], writes=[dxt])
                        psm = b.take_psum()
                        s.op("tensor", lambda e, g=g, tc=tc, ps=psm: e.matmul(
                            ps[:, :128], Bc[g][:, tc * 128:(tc + 1) * 128],
                            Cc[g][:, tc * 128:(tc + 1) * 128], start=True, stop=True),
                            reads=[Bc[g], Cc[g]], writes=[psm])
                        psl = b.take_psum()
                        s.op("tensor", lambda e, tc=tc, h=h, ps=psl: e.transpose(
                            ps[0:1, :128], TT[tc][:, h:h + 1], identf[:]),
                            reads=[TT[tc], identf], writes=[psl])
                        s.op("vector", lambda e, lr=lr, ps=psl: e.tensor_copy(
                            out=lr[:], in_=ps[0:1, :128]), reads=[psl], writes=[lr])
                        psb = b.take_psum()
                        s.op("tensor", lambda e, lr=lr, ps=psb: e.matmul(
                            ps[:, :128], o_row[:], lr[:], start=True, stop=True),
                            reads=[o_row, lr], writes=[psb])
                        s.op("vector", lambda e, dp=dp, tc=tc, h=h, ps=psb: e.tensor_scalar(
                            out=dp[:], in0=ps[:, :128], scalar1=TT[tc][:, h:h + 1],
                            scalar2=None, op0=ALU.subtract), reads=[psb, TT[tc]], writes=[dp])
                        s.op("gpsimd", lambda e, dp=dp: e.tensor_add(
                            out=dp[:], in0=dp[:], in1=trimask[:]),
                            reads=[dp, trimask], writes=[dp])
                        s.op("scalar", lambda e, ep2=ep2, dp=dp: e.activation(
                            out=ep2[:], in_=dp[:], func=AF.Exp), reads=[dp], writes=[ep2])
                        s.op("vector", lambda e, wp=wp, ep2=ep2, ps=psm: e.tensor_tensor(
                            out=wp[:], in0=ep2[:], in1=ps[:, :128], op=ALU.mult),
                            reads=[ep2, psm], writes=[wp])
                        p1 = b.take_psum()
                        s.op("tensor", lambda e, wp=wp, dxt=dxt, ps=p1: e.matmul(
                            ps[:, :64], wp[:], dxt[:], start=True, stop=True),
                            reads=[wp, dxt], writes=[p1])
                        p2 = b.take_psum()
                        s.op("tensor", lambda e, g=g, tc=tc, h=h, ps=p2: e.matmul(
                            ps[:, :64], Cc[g][:, tc * 128:(tc + 1) * 128], St[h][:],
                            start=True, stop=True), reads=[Cc[g], St[h]], writes=[p2])
                        s.op("vector", lambda e, yt=yt, xt=xt, h=h, ps=p1: e.scalar_tensor_tensor(
                            out=yt[:], in0=xt[:], scalar=dbc_t[:, h:h + 1], in1=ps[:, :64],
                            op0=ALU.mult, op1=ALU.add), reads=[xt, dbc_t, p1], writes=[yt])
                        s.op("vector", lambda e, yo=yo, yt=yt, tc=tc, h=h, ps=p2: e.scalar_tensor_tensor(
                            out=yo[:], in0=ps[:, :64], scalar=TT[tc][:, 32 + h:33 + h],
                            in1=yt[:], op0=ALU.mult, op1=ALU.add),
                            reads=[p2, TT[tc], yt], writes=[yo])
                        s.op("gpsimd", lambda e, yo=yo, tc=tc, h=h: e.dma_start(
                            out=d_y[tc * 128:(tc + 1) * 128, h * 64:(h + 1) * 64], in_=yo[:]),
                            reads=[yo], writes=[d_y], dma=True)
                        psbt = b.take_psum()
                        s.op("tensor", lambda e, g=g, tc=tc, ps=psbt: e.transpose(
                            ps[:, :128], Bc[g][:, tc * 128:(tc + 1) * 128], identf[:]),
                            reads=[Bc[g], identf], writes=[psbt])
                        s.op("vector", lambda e, btm=btm, ps=psbt: e.tensor_copy(
                            out=btm[:], in_=ps[:, :128]), reads=[psbt], writes=[btm])
                        s.op("vector", lambda e, bwt=bwt, btm=btm, tc=tc, h=h: e.tensor_scalar(
                            out=bwt[:], in0=btm[:], scalar1=TT[tc][:, 64 + h:65 + h],
                            scalar2=None, op0=ALU.mult), reads=[btm, TT[tc]], writes=[bwt])
                        p3 = b.take_psum()
                        s.op("tensor", lambda e, bwt=bwt, dxt=dxt, ps=p3: e.matmul(
                            ps[:, :64], bwt[:], dxt[:], start=True, stop=True),
                            reads=[bwt, dxt], writes=[p3])
                        s.op("vector", lambda e, h=h, tc=tc, ps=p3: e.scalar_tensor_tensor(
                            out=St[h][:], in0=St[h][:], scalar=gebc[:, h * 16 + tc:h * 16 + tc + 1],
                            in1=ps[:, :64], op0=ALU.mult, op1=ALU.add),
                            reads=[St[h], gebc, p3], writes=[St[h]])
            s.finalize(block, final_waits=[s.last_writer[id(d_y)]])
    return b.nc



def build_mtail_launch():
    b = Builder("mtail")
    nc = b.nc
    d_x = nc.dram_tensor("x", [E, TPC], F32, kind="ExternalInput").ap()
    d_y = nc.dram_tensor("y", [E, TPC], F32, kind="ExternalInput").ap()
    d_gs = nc.dram_tensor("gs", [E, TPC], BF16, kind="ExternalInput").ap()
    d_nw = nc.dram_tensor("nw", [E, 1], F32, kind="ExternalInput").ap()
    d_wo = nc.dram_tensor("woT", [E, E], F32, kind="ExternalInput").ap()
    d_out = nc.dram_tensor("out", [E, TPC], F32, kind="ExternalOutput").ap()

    with b as b:
        s = b.seq
        b.c_ones = ones_consts(b)
        o_col, o_row, _, _ = b.c_ones
        yn = [[b.sbuf([128, 512], BF16, f"yn{kc}_{h}") for h in range(2)] for kc in range(32)]
        y_t = b.sbufs(2, [128, 512], F32, "y_t")
        sq_t = b.sbufs(2, [128, 512], F32, "sq_t")
        gs_t = b.sbufs(2, [128, 512], BF16, "gs_t")
        nw_t = b.sbuf([128, 32], F32, "nw_t")
        stat = b.sbuf([1, 512], F32, "stat")
        rbc = b.sbuf([128, 512], F32, "rbc")
        t1 = b.sbufs(2, [128, 512], F32, "t1")
        xr = b.sbufs(2, [128, 512], F32, "xr")
        ot = b.sbufs(2, [128, 512], F32, "ot")
        epst = None

        with nc.Block() as block:
            for c in range(32):
                s.op("sync", lambda e, c=c: e.dma_start(
                    out=nw_t[:, c:c + 1], in_=d_nw[c * 128:(c + 1) * 128, :]),
                    reads=[d_nw], writes=[nw_t], dma=True)
            epst = b.const_tile(1e-5)
            for hh in range(2):
                for g in range(8):
                    ps_q = b.take_psum()
                    for i in range(4):
                        kc = g * 4 + i
                        yt = y_t[i % 2]
                        st = sq_t[i % 2]
                        s.op("sync", lambda e, kc=kc, hh=hh, yt=yt: e.dma_start(
                            out=yt[:], in_=d_y[kc * 128:(kc + 1) * 128, hh * 512:(hh + 1) * 512]),
                            reads=[d_y], writes=[yt], dma=True)
                        s.op("scalar", lambda e, st=st, yt=yt: e.activation(
                            out=st[:], in_=yt[:], func=AF.Square), reads=[yt], writes=[st])
                        s.op("tensor", lambda e, st=st, i=i: e.matmul(
                            ps_q[0:1, :], o_col[:], st[:], start=(i == 0), stop=(i == 3)),
                            reads=[st, o_col], writes=[ps_q])
                    s.op("scalar", lambda e: e.activation(
                        out=stat[:], in_=ps_q[0:1, :], func=AF.Copy, scale=1.0 / 512.0),
                        reads=[ps_q], writes=[stat])
                    s.op("scalar", lambda e: e.activation(
                        out=stat[:], in_=stat[:], func=AF.Sqrt, bias=epst[:]),
                        reads=[stat, epst], writes=[stat])
                    s.op("vector", lambda e: e.reciprocal(out=stat[:], in_=stat[:]),
                         reads=[stat], writes=[stat])
                    ps_b = b.take_psum()
                    s.op("tensor", lambda e, ps=ps_b: e.matmul(
                        ps[:], o_row[:], stat[:], start=True, stop=True),
                        reads=[o_row, stat], writes=[ps_b])
                    s.op("vector", lambda e, ps=ps_b: e.tensor_copy(out=rbc[:], in_=ps[:]),
                         reads=[ps_b], writes=[rbc])
                    for i in range(4):
                        kc = g * 4 + i
                        yt = y_t[i % 2]
                        tt = t1[i % 2]
                        gt = gs_t[i % 2]
                        s.op("sync", lambda e, kc=kc, hh=hh, yt=yt: e.dma_start(
                            out=yt[:], in_=d_y[kc * 128:(kc + 1) * 128, hh * 512:(hh + 1) * 512]),
                            reads=[d_y], writes=[yt], dma=True)
                        s.op("sync", lambda e, kc=kc, hh=hh, gt=gt: e.dma_start(
                            out=gt[:], in_=d_gs[kc * 128:(kc + 1) * 128, hh * 512:(hh + 1) * 512]),
                            reads=[d_gs], writes=[gt], dma=True)
                        s.op("vector", lambda e, yt=yt, tt=tt, kc=kc: e.scalar_tensor_tensor(
                            out=tt[:], in0=yt[:], scalar=nw_t[:, kc:kc + 1], in1=rbc[:],
                            op0=ALU.mult, op1=ALU.mult), reads=[yt, nw_t, rbc], writes=[tt])
                        s.op("vector", lambda e, tt=tt, gt=gt, kc=kc, hh=hh: e.tensor_tensor(
                            out=yn[kc][hh][:], in0=tt[:], in1=gt[:], op=ALU.mult),
                            reads=[tt, gt], writes=[yn[kc][hh]])

            def epi(oc, hh, psx):
                xrt, ott = xr[hh], ot[hh]
                s.op("sync", lambda e, oc=oc, hh=hh, xrt=xrt: e.dma_start(
                    out=xrt[:], in_=d_x[oc * 128:(oc + 1) * 128, hh * 512:(hh + 1) * 512]),
                    reads=[d_x], writes=[xrt], dma=True)
                s.op("vector", lambda e, ps=psx, xrt=xrt, ott=ott: e.tensor_add(
                    out=ott[:], in0=ps[:], in1=xrt[:]), reads=[psx, xrt], writes=[ott])
                s.op("gpsimd", lambda e, oc=oc, hh=hh, ott=ott: e.dma_start(
                    out=d_out[oc * 128:(oc + 1) * 128, hh * 512:(hh + 1) * 512], in_=ott[:]),
                    reads=[ott], writes=[d_out], dma=True)

            emit_proj_stream(b, d_wo, range(32), yn, epi, wtag="w")
            s.finalize(block, final_waits=[s.last_writer[id(d_out)]])
    return b.nc


def run_mamba(x_fm, pre, cw, cb, dtb, alog, Dv, nw, out_proj_w, in_proj_w):
    mf = run_mfront(x_fm, in_proj_w, dtb)
    nc = build_ssd_launch()
    in_maps = []
    for c in range(NCORES):
        bb, par = c // 2, c % 2
        h0 = par * 32
        hbc_b = np.concatenate([mf["hbc"][:, (2 * bb) * TPC:(2 * bb + 1) * TPC],
                                mf["hbc"][:, (2 * bb + 1) * TPC:(2 * bb + 2) * TPC]], axis=1)
        dtp_b = np.concatenate([mf["dtp"][:, (2 * bb) * TPC:(2 * bb + 1) * TPC],
                                mf["dtp"][:, (2 * bb + 1) * TPC:(2 * bb + 2) * TPC]], axis=1)
        hx = np.zeros((2048, S + 3), np.float32)
        hx[:, 3:] = hbc_b[h0 * 64:h0 * 64 + 2048]
        bc = np.zeros((1024, S + 3), np.float32)
        g0 = par * 4
        bc[:512, 3:] = hbc_b[4096 + g0 * 128:4096 + g0 * 128 + 512]
        bc[512:, 3:] = hbc_b[5120 + g0 * 128:5120 + g0 * 128 + 512]
        in_maps.append({
            "hx": hx, "bc": np.ascontiguousarray(bc),
            "cwx": np.ascontiguousarray(cw[h0 * 64:h0 * 64 + 2048]),
            "cbx": np.ascontiguousarray(cb[h0 * 64:h0 * 64 + 2048].reshape(-1, 1)),
            "cwb": np.ascontiguousarray(np.concatenate(
                [cw[4096 + g0 * 128:4096 + g0 * 128 + 512],
                 cw[5120 + g0 * 128:5120 + g0 * 128 + 512]])),
            "cbb": np.ascontiguousarray(np.concatenate(
                [cb[4096 + g0 * 128:4096 + g0 * 128 + 512],
                 cb[5120 + g0 * 128:5120 + g0 * 128 + 512]]).reshape(-1, 1)),
            "dtp": np.ascontiguousarray(dtp_b[h0:h0 + 32]),
            "alog": np.ascontiguousarray(alog[h0:h0 + 32].reshape(32, 1)),
            "dbc": np.ascontiguousarray(np.tile(Dv[h0:h0 + 32], (128, 1))),
        })
    res = run_bass_kernel_spmd(nc, in_maps, core_ids=list(range(NCORES)))
    y_fm = np.zeros((E, TOK), np.float32)
    for c in range(NCORES):
        bb, par = c // 2, c % 2
        yt = res.results[c]["y"]
        y_fm[par * 2048:(par + 1) * 2048, bb * S:(bb + 1) * S] = yt.T
    nc2 = build_mtail_launch()
    woT = np.ascontiguousarray(out_proj_w.T)
    in_maps2 = []
    for c in range(NCORES):
        in_maps2.append({
            "x": np.ascontiguousarray(x_fm[:, c * TPC:(c + 1) * TPC]),
            "y": np.ascontiguousarray(y_fm[:, c * TPC:(c + 1) * TPC]),
            "gs": np.ascontiguousarray(mf["gs"][:, c * TPC:(c + 1) * TPC]),
            "nw": np.ascontiguousarray(nw.reshape(E, 1)),
            "woT": woT,
        })
    res2 = run_bass_kernel_spmd(nc2, in_maps2, core_ids=list(range(NCORES)))
    return np.concatenate([res2.results[c]["out"] for c in range(NCORES)], axis=1)



def build_proj_resid_launch(k_dim, with_bias, name):
    b = Builder(name)
    nc = b.nc
    d_x = nc.dram_tensor("x", [E, TPC], F32, kind="ExternalInput").ap()
    d_a = nc.dram_tensor("a", [k_dim, TPC], BF16, kind="ExternalInput").ap()
    d_w = nc.dram_tensor("wT", [k_dim, E], F32, kind="ExternalInput").ap()
    if with_bias:
        d_bo = nc.dram_tensor("bo", [E, 1], F32, kind="ExternalInput").ap()
    d_out = nc.dram_tensor("out", [E, TPC], F32, kind="ExternalOutput").ap()
    nk = k_dim // 128

    with b as b:
        s = b.seq
        b.c_ones = ones_consts(b)
        at = [[b.sbuf([128, 512], BF16, f"at{kc}_{h}") for h in range(2)] for kc in range(nk)]
        xr = b.sbufs(2, [128, 512], F32, "xr")
        ot = b.sbufs(2, [128, 512], F32, "ot")
        if with_bias:
            bo_t = b.sbuf([128, 32], F32, "bo_t")

        with nc.Block() as block:
            if with_bias:
                for c in range(32):
                    s.op("sync", lambda e, c=c: e.dma_start(
                        out=bo_t[:, c:c + 1], in_=d_bo[c * 128:(c + 1) * 128, :]),
                        reads=[d_bo], writes=[bo_t], dma=True)
            for kc in range(nk):
                for h in range(2):
                    s.op("sync", lambda e, kc=kc, h=h: e.dma_start(
                        out=at[kc][h][:], in_=d_a[kc * 128:(kc + 1) * 128, h * 512:(h + 1) * 512]),
                        reads=[d_a], writes=[at[kc][h]], dma=True)

            def epi(oc, hh, psx):
                xrt, ott = xr[hh], ot[hh]
                s.op("sync", lambda e, oc=oc, hh=hh, xrt=xrt: e.dma_start(
                    out=xrt[:], in_=d_x[oc * 128:(oc + 1) * 128, hh * 512:(hh + 1) * 512]),
                    reads=[d_x], writes=[xrt], dma=True)
                if with_bias:
                    s.op("vector", lambda e, ps=psx, xrt=xrt, ott=ott, oc=oc: e.scalar_tensor_tensor(
                        out=ott[:], in0=ps[:], scalar=bo_t[:, oc:oc + 1], in1=xrt[:],
                        op0=ALU.add, op1=ALU.add), reads=[psx, bo_t, xrt], writes=[ott])
                else:
                    s.op("vector", lambda e, ps=psx, xrt=xrt, ott=ott: e.tensor_add(
                        out=ott[:], in0=ps[:], in1=xrt[:]), reads=[psx, xrt], writes=[ott])
                s.op("gpsimd", lambda e, oc=oc, hh=hh, ott=ott: e.dma_start(
                    out=d_out[oc * 128:(oc + 1) * 128, hh * 512:(hh + 1) * 512], in_=ott[:]),
                    reads=[ott], writes=[d_out], dma=True)

            emit_proj_stream(b, d_w, range(32), at, epi, n_k=nk, wtag="w")
            s.finalize(block, final_waits=[s.last_writer[id(d_out)]])
    return b.nc


def run_proj_resid(x_fm, a_fm, wT, bo=None, name="projres"):
    nc = build_proj_resid_launch(a_fm.shape[0], bo is not None, name)
    in_maps = []
    for c in range(NCORES):
        m = {"x": np.ascontiguousarray(x_fm[:, c * TPC:(c + 1) * TPC]),
             "a": np.ascontiguousarray(a_fm[:, c * TPC:(c + 1) * TPC]),
             "wT": wT}
        if bo is not None:
            m["bo"] = np.ascontiguousarray(bo.reshape(E, 1))
        in_maps.append(m)
    res = run_bass_kernel_spmd(nc, in_maps, core_ids=list(range(NCORES)))
    return np.concatenate([res.results[c]["out"] for c in range(NCORES)], axis=1)



def build_qkv1_launch():
    b = Builder("qkv1")
    nc = b.nc
    d_x = nc.dram_tensor("x", [E, TPC], F32, kind="ExternalInput").ap()
    d_w = nc.dram_tensor("wT", [E, 3072], F32, kind="ExternalInput").ap()
    d_bias = nc.dram_tensor("bias", [3072, 1], F32, kind="ExternalInput").ap()
    d_q = nc.dram_tensor("q", [1024, TPC], BF16, kind="ExternalOutput").ap()
    d_k = nc.dram_tensor("k", [1024, TPC], BF16, kind="ExternalOutput").ap()
    d_v = nc.dram_tensor("v", [1024, TPC], BF16, kind="ExternalOutput").ap()

    with b as b:
        s = b.seq
        b.c_ones = ones_consts(b)
        xln = [[b.sbuf([128, 512], BF16, f"xln{kc}_{h}") for h in range(2)] for kc in range(32)]
        bias_t = b.sbuf([128, 24], F32, "bias_t")
        ev = b.sbufs(2, [128, TPC], BF16, "ev")

        with nc.Block() as block:
            for c in range(24):
                s.op("sync", lambda e, c=c: e.dma_start(
                    out=bias_t[:, c:c + 1], in_=d_bias[c * 128:(c + 1) * 128, :]),
                    reads=[d_bias], writes=[bias_t], dma=True)
            for h in range(2):
                emit_ln_fm(b, d_x, h * 512, [xln[kc][h] for kc in range(32)])

            def epi(oc, hh, psx):
                et = ev[oc % 2]
                s.op("vector", lambda e, oc=oc, hh=hh, ps=psx, et=et: e.tensor_scalar(
                    out=et[:, hh * 512:(hh + 1) * 512], in0=ps[:],
                    scalar1=bias_t[:, oc:oc + 1], scalar2=None, op0=ALU.add),
                    reads=[psx, bias_t], writes=[et])
                if hh == 1:
                    dst = (d_q, d_k, d_v)[oc // 8]
                    roc = oc % 8
                    s.op("gpsimd", lambda e, et=et, dst=dst, roc=roc: e.dma_start(
                        out=dst[roc * 128:(roc + 1) * 128, :], in_=et[:]),
                        reads=[et], writes=[dst], dma=True)

            emit_proj_stream(b, d_w, range(24), xln, epi, wtag="w")
            s.finalize(block, final_waits=[s.last_writer[id(d_q)],
                                           s.last_writer[id(d_k)],
                                           s.last_writer[id(d_v)]])
    return b.nc



def build_attn1_launch():
    b = Builder("attn1", npsum=7)
    nc = b.nc
    d_q = nc.dram_tensor("q", [1024, 1024], BF16, kind="ExternalInput").ap()
    d_k = nc.dram_tensor("k", [1024, S], BF16, kind="ExternalInput").ap()
    d_v = nc.dram_tensor("v", [1024, S], BF16, kind="ExternalInput").ap()
    d_msk = nc.dram_tensor("msk", [128, NSLOT * 256], F32, kind="ExternalInput").ap()
    d_ao = nc.dram_tensor("ao", [1024, 1024], BF16, kind="ExternalOutput").ap()
    SCL1 = 1.0 / 32.0

    with b as b:
        s = b.seq
        b.c_ones = ones_consts(b)
        ps_bf = b.ctx.enter_context(nc.psum_tensor("ps_bf", [128, 1024], BF16))
        identb = b.sbuf([128, 128], BF16, "identb")
        msk_t = b.sbuf([128, NSLOT * 256], F32, "msk_t")
        q8 = [b.sbuf([128, 1024], BF16, f"q8_{d}") for d in range(8)]
        k_t = [b.sbuf([128, S], BF16, f"k_t{d}") for d in range(8)]
        v_tm = [b.sbuf([128, S], BF16, f"v_tm{d}") for d in range(8)]
        v_t = b.sbufs(2, [128, S], BF16, "v_t")
        msc = b.sbufs(2, [128, S], F32, "msc")
        att = b.sbufs(2, [128, S], BF16, "att")
        atn = b.sbufs(2, [128, S], BF16, "atn")
        atT = b.sbufs(2, [128, S], BF16, "atT")
        nmx = b.sbuf([128, 2], F32, "nmx")
        eb = b.sbuf([128, 2], F32, "eb")
        se = b.sbuf([128, 2], F32, "se")
        rc = b.sbuf([128, 2], F32, "rc")
        oev = b.sbufs(2, [128, 128], BF16, "oev")

        with nc.Block() as block:
            s.op("gpsimd", lambda e: e.memset(identb[:], 1.0), writes=[identb])
            s.op("gpsimd", lambda e: e.affine_select(
                out=identb[:], in_=identb[:], compare_op=ALU.is_equal, fill=0.0,
                base=0, pattern=[[-1, 128]], channel_multiplier=1),
                reads=[identb], writes=[identb])
            s.op("sync", lambda e: e.dma_start(out=msk_t[:], in_=d_msk), reads=[d_msk],
                 writes=[msk_t], dma=True)
            for d in range(8):
                s.op("sync", lambda e, d=d: e.dma_start(
                    out=q8[d][:], in_=d_q[d * 128:(d + 1) * 128, :]),
                    reads=[d_q], writes=[q8[d]], dma=True)
                s.op("sync", lambda e, d=d: e.dma_start(
                    out=k_t[d][:], in_=d_k[d * 128:(d + 1) * 128, :]),
                    reads=[d_k], writes=[k_t[d]], dma=True)
                vt = v_t[d % 2]
                s.op("sync", lambda e, d=d, vt=vt: e.dma_start(
                    out=vt[:], in_=d_v[d * 128:(d + 1) * 128, :]),
                    reads=[d_v], writes=[vt], dma=True)
                for g2 in range(2):
                    for j in range(8):
                        kc = g2 * 8 + j
                        s.op("tensor", lambda e, kc=kc, j=j, vt=vt: e.transpose(
                            ps_bf[:, j * 128:(j + 1) * 128],
                            vt[:, kc * 128:(kc + 1) * 128], identb[:]),
                            reads=[vt, identb], writes=[ps_bf])
                    s.op("vector", lambda e, d=d, g2=g2: e.tensor_copy(
                        out=v_tm[d][:, g2 * 1024:(g2 + 1) * 1024], in_=ps_bf[:]),
                        reads=[ps_bf], writes=[v_tm[d]])

            for sl in range(NSLOT):
                u = sl % 2
                nk = slot_nk(sl)
                nch = nk // 128
                msct, attt, atnt, atTt = msc[u], att[u], atn[u], atT[u]
                for ck in range(ceil_div(nk, 512)):
                    w = min(512, nk - ck * 512)
                    psx = b.take_psum()
                    for dd in range(8):
                        s.op("tensor", lambda e, sl=sl, ck=ck, w=w, dd=dd, ps=psx: e.matmul(
                            ps[:, :w], q8[dd][:, sl * 128:(sl + 1) * 128],
                            k_t[dd][:, ck * 512:ck * 512 + w],
                            start=(dd == 0), stop=(dd == 7)),
                            reads=[q8[dd], k_t[dd]], writes=[psx])
                    s.op("scalar", lambda e, ck=ck, w=w, ps=psx, msct=msct: e.copy(
                        out=msct[:, ck * 512:ck * 512 + w], in_=ps[:, :w]),
                        reads=[psx], writes=[msct])
                s.op("vector", lambda e, sl=sl, nk=nk, msct=msct: e.tensor_add(
                    out=msct[:, nk - 256:nk], in0=msct[:, nk - 256:nk],
                    in1=msk_t[:, sl * 256:(sl + 1) * 256]),
                    reads=[msct, msk_t], writes=[msct])
                s.op("vector", lambda e, u=u, nk=nk, msct=msct: e.reduce_max(
                    out=nmx[:, u:u + 1], in_=msct[:, :nk], axis=AX.X, negate=True),
                    reads=[msct], writes=[nmx])
                s.op("vector", lambda e, u=u: e.tensor_scalar(
                    out=eb[:, u:u + 1], in0=nmx[:, u:u + 1], scalar1=SCL1, scalar2=None,
                    op0=ALU.mult), reads=[nmx], writes=[eb])
                s.op("scalar", lambda e, u=u, nk=nk, msct=msct, attt=attt: e.activation(
                    out=attt[:, :nk], in_=msct[:, :nk], func=AF.Exp, scale=SCL1,
                    bias=eb[:, u:u + 1], accum_out=se[:, u:u + 1]),
                    reads=[msct, eb], writes=[attt, se])
                s.op("vector", lambda e, u=u: e.reciprocal(out=rc[:, u:u + 1], in_=se[:, u:u + 1]),
                     reads=[se], writes=[rc])
                s.op("vector", lambda e, u=u, nk=nk, attt=attt, atnt=atnt: e.tensor_scalar(
                    out=atnt[:, :nk], in0=attt[:, :nk], scalar1=rc[:, u:u + 1],
                    scalar2=None, op0=ALU.mult), reads=[attt, rc], writes=[atnt])
                for g2 in range(ceil_div(nch, 8)):
                    jj0, jj1 = g2 * 8, min(g2 * 8 + 8, nch)
                    for jj in range(jj0, jj1):
                        s.op("tensor", lambda e, jj=jj, atnt=atnt: e.transpose(
                            ps_bf[:, (jj % 8) * 128:(jj % 8 + 1) * 128],
                            atnt[:, jj * 128:(jj + 1) * 128], identb[:]),
                            reads=[atnt, identb], writes=[ps_bf])
                    s.op("vector", lambda e, jj0=jj0, jj1=jj1, atTt=atTt: e.tensor_copy(
                        out=atTt[:, jj0 * 128:jj1 * 128], in_=ps_bf[:, :(jj1 - jj0) * 128]),
                        reads=[ps_bf], writes=[atTt])
                for dv in range(8):
                    psx = b.take_psum()
                    for kc in range(nch):
                        s.op("tensor", lambda e, dv=dv, kc=kc, nch=nch, ps=psx, atTt=atTt:
                             e.matmul(ps[:, :128], v_tm[dv][:, kc * 128:(kc + 1) * 128],
                             atTt[:, kc * 128:(kc + 1) * 128],
                             start=(kc == 0), stop=(kc == nch - 1)),
                             reads=[v_tm[dv], atTt], writes=[psx])
                    ov = oev[dv % 2]
                    s.op("scalar", lambda e, ps=psx, ov=ov: e.copy(
                        out=ov[:], in_=ps[:, :128]), reads=[psx], writes=[ov])
                    s.op("gpsimd", lambda e, dv=dv, sl=sl, ov=ov: e.dma_start(
                        out=d_ao[dv * 128:(dv + 1) * 128, sl * 128:(sl + 1) * 128], in_=ov[:]),
                        reads=[ov], writes=[d_ao], dma=True)
            s.finalize(block, final_waits=[s.last_writer[id(d_ao)]])
    return b.nc


def run_l1_attn(x_fm, Wqkv, bqkv, Wo, bo):
    nc = build_qkv1_launch()
    in_maps = []
    for c in range(NCORES):
        in_maps.append({
            "x": np.ascontiguousarray(x_fm[:, c * TPC:(c + 1) * TPC]),
            "wT": np.ascontiguousarray(Wqkv),
            "bias": np.ascontiguousarray(bqkv.reshape(3072, 1)),
        })
    res = run_bass_kernel_spmd(nc, in_maps, core_ids=list(range(NCORES)))
    q = np.concatenate([res.results[c]["q"] for c in range(NCORES)], axis=1)
    k = np.concatenate([res.results[c]["k"] for c in range(NCORES)], axis=1)
    v = np.concatenate([res.results[c]["v"] for c in range(NCORES)], axis=1)
    nc2 = build_attn1_launch()
    in_maps2 = []
    colss = []
    for c in range(NCORES):
        bb, par = c // 2, c % 2
        tiles = [2 * i + par for i in range(NSLOT)]
        tok0 = bb * S
        cols = np.concatenate([np.arange(tok0 + t * 128, tok0 + (t + 1) * 128) for t in tiles])
        msk = np.zeros((NSLOT, 128, 256), np.float32)
        for i, t in enumerate(tiles):
            svec = t * 128 + np.arange(128)
            nk = slot_nk(i)
            jj = np.arange(nk - 256, nk)
            msk[i] = np.where(jj[None, :] <= svec[:, None], 0.0, -1e30)
        in_maps2.append({
            "q": np.ascontiguousarray(q[:, cols]),
            "k": np.ascontiguousarray(k[:, tok0:tok0 + S]),
            "v": np.ascontiguousarray(v[:, tok0:tok0 + S]),
            "msk": np.ascontiguousarray(msk.transpose(1, 0, 2).reshape(128, NSLOT * 256)),
        })
        colss.append(cols)
    res2 = run_bass_kernel_spmd(nc2, in_maps2, core_ids=list(range(NCORES)))
    ao = np.zeros((1024, TOK), dtype=BF)
    for c in range(NCORES):
        ao[:, colss[c]] = res2.results[c]["ao"]
    return run_proj_resid(x_fm, ao, np.ascontiguousarray(Wo), bo=bo, name="wo1")



def build_moe_launch():
    b = Builder("moe")
    nc = b.nc
    d_x = nc.dram_tensor("x", [E, TPC], F32, kind="ExternalInput").ap()
    d_xtm = nc.dram_tensor("xtm", [TPC, E], F32, kind="ExternalInput").ap()
    d_at = nc.dram_tensor("aT", [4, E, 16], F32, kind="ExternalInput").ap()
    d_bt = nc.dram_tensor("bT", [4, 16, E], F32, kind="ExternalInput").ap()
    d_wr = nc.dram_tensor("wrT", [E, 4], F32, kind="ExternalInput").ap()
    d_out = nc.dram_tensor("out", [TPC, E], F32, kind="ExternalOutput").ap()
    MSCALE = 4.0

    with b as b:
        s = b.seq
        b.c_ones = ones_consts(b)
        identf = b.sbuf([128, 128], F32, "identf")
        xln = [[b.sbuf([128, 512], BF16, f"xln{kc}_{h}") for h in range(2)] for kc in range(32)]
        wr_t = b.sbuf([128, E // 128 * 4], F32, "wr_t")
        wr_b = b.sbuf([128, E // 128 * 4], BF16, "wr_b")
        at_f = b.sbuf([128, 512], F32, "at_f")
        at_b = [b.sbuf([128, 16 * 32], BF16, f"at_b{e}") for e in range(4)]
        bt_f = b.sbuf([16, E], F32, "bt_f")
        bt_b = [b.sbuf([16, E], BF16, f"bt_b{e}") for e in range(4)]
        lg_f = b.sbuf([4, TPC], F32, "lg_f")
        l_tm = b.sbuf([128, 4 * 8], F32, "l_tm")
        mid_b = [b.sbuf([16, TPC], BF16, f"mid_b{e}") for e in range(4)]
        g_tm = b.sbuf([128, 4 * 8], F32, "g_tm")
        m1 = b.sbuf([128, 8], F32, "m1")
        m2 = b.sbuf([128, 8], F32, "m2")
        eq1 = b.sbuf([128, 4], F32, "eq1")
        l2t = b.sbuf([128, 4], F32, "l2t")
        ind = b.sbuf([128, 4], F32, "ind")
        ee = b.sbuf([128, 4], F32, "ee")
        zz = b.sbuf([128, 8], F32, "zz")
        acc = b.sbufs(1, [128, E], F32, "acc")
        xtm_t = b.sbufs(1, [128, E], F32, "xtm_t")
        out_t = b.sbufs(1, [128, E], F32, "out_t")

        with nc.Block() as block:
            s.op("gpsimd", lambda e: e.memset(identf[:], 1.0), writes=[identf])
            s.op("gpsimd", lambda e: e.affine_select(
                out=identf[:], in_=identf[:], compare_op=ALU.is_equal, fill=0.0,
                base=0, pattern=[[-1, 128]], channel_multiplier=1),
                reads=[identf], writes=[identf])
            s.op("sync", lambda e: e.dma_start(
                out=wr_t[:], in_=d_wr.rearrange("(kc p) m -> p kc m", p=128)),
                reads=[d_wr], writes=[wr_t], dma=True)
            s.op("vector", lambda e: e.tensor_copy(out=wr_b[:], in_=wr_t[:]),
                 reads=[wr_t], writes=[wr_b])
            for ex in range(4):
                s.op("sync", lambda e, ex=ex: e.dma_start(
                    out=at_f[:], in_=d_at[ex].rearrange("(kc p) m -> p kc m", p=128)),
                    reads=[d_at], writes=[at_f], dma=True)
                s.op("vector", lambda e, ex=ex: e.tensor_copy(out=at_b[ex][:], in_=at_f[:]),
                     reads=[at_f], writes=[at_b[ex]])
                s.op("sync", lambda e, ex=ex: e.dma_start(out=bt_f[:], in_=d_bt[ex]),
                     reads=[d_bt], writes=[bt_f], dma=True)
                s.op("vector", lambda e, ex=ex: e.tensor_copy(out=bt_b[ex][:], in_=bt_f[:]),
                     reads=[bt_f], writes=[bt_b[ex]])
            for h in range(2):
                emit_ln_fm(b, d_x, h * 512, [xln[kc][h] for kc in range(32)])
            for hh in range(2):
                psx = b.take_psum()
                for kc in range(32):
                    s.op("tensor", lambda e, kc=kc, hh=hh, ps=psx: e.matmul(
                        ps[0:4, :], wr_b[:, kc * 4:(kc + 1) * 4], xln[kc][hh][:],
                        start=(kc == 0), stop=(kc == 31)),
                        reads=[wr_b, xln[kc][hh]], writes=[psx])
                s.op("vector", lambda e, hh=hh, ps=psx: e.tensor_copy(
                    out=lg_f[:, hh * 512:(hh + 1) * 512], in_=ps[0:4, :]),
                    reads=[psx], writes=[lg_f])
            for ex in range(4):
                for hh in range(2):
                    psx = b.take_psum()
                    for kc in range(32):
                        s.op("tensor", lambda e, ex=ex, kc=kc, hh=hh, ps=psx: e.matmul(
                            ps[0:16, :], at_b[ex][:, kc * 16:(kc + 1) * 16], xln[kc][hh][:],
                            start=(kc == 0), stop=(kc == 31)),
                            reads=[at_b[ex], xln[kc][hh]], writes=[psx])
                    s.op("vector", lambda e, ex=ex, hh=hh, ps=psx: e.tensor_copy(
                        out=mid_b[ex][:, hh * 512:(hh + 1) * 512], in_=ps[0:16, :]),
                        reads=[psx], writes=[mid_b[ex]])
            for tch in range(8):
                psx = b.take_psum()
                s.op("tensor", lambda e, tch=tch, ps=psx: e.transpose(
                    ps[:, :4], lg_f[:, tch * 128:(tch + 1) * 128], identf[:4, :4]),
                    reads=[lg_f, identf], writes=[psx])
                s.op("vector", lambda e, tch=tch, ps=psx: e.tensor_copy(
                    out=l_tm[:, tch * 4:(tch + 1) * 4], in_=ps[:, :4]),
                    reads=[psx], writes=[l_tm])
                lt = l_tm[:, tch * 4:(tch + 1) * 4]
                s.op("vector", lambda e, tch=tch, lt=lt: e.reduce_max(
                    out=m1[:, tch:tch + 1], in_=lt, axis=AX.X, negate=True),
                    reads=[l_tm], writes=[m1])
                s.op("vector", lambda e, tch=tch, lt=lt: e.tensor_scalar(
                    out=eq1[:], in0=lt, scalar1=m1[:, tch:tch + 1], scalar2=-1.0,
                    op0=ALU.add, op1=ALU.mult), reads=[l_tm, m1], writes=[eq1])
                s.op("vector", lambda e: e.tensor_scalar(
                    out=l2t[:], in0=eq1[:], scalar1=0.0, scalar2=None, op0=ALU.is_equal),
                    reads=[eq1], writes=[l2t])
                s.op("vector", lambda e, tch=tch, lt=lt: e.scalar_tensor_tensor(
                    out=l2t[:], in0=l2t[:], scalar=-1e30, in1=lt,
                    op0=ALU.mult, op1=ALU.add), reads=[l2t, l_tm], writes=[l2t])
                s.op("vector", lambda e, tch=tch: e.reduce_max(
                    out=m2[:, tch:tch + 1], in_=l2t[:], axis=AX.X, negate=False),
                    reads=[l2t], writes=[m2])
                s.op("vector", lambda e, tch=tch, lt=lt: e.tensor_scalar(
                    out=ind[:], in0=lt, scalar1=m2[:, tch:tch + 1], scalar2=None,
                    op0=ALU.is_ge), reads=[l_tm, m2], writes=[ind])
                s.op("scalar", lambda e, tch=tch, lt=lt: e.activation(
                    out=ee[:], in_=lt, func=AF.Exp, bias=m1[:, tch:tch + 1]),
                    reads=[l_tm, m1], writes=[ee])
                s.op("vector", lambda e: e.tensor_mul(out=ee[:], in0=ee[:], in1=ind[:]),
                     reads=[ee, ind], writes=[ee])
                s.op("vector", lambda e, tch=tch: e.tensor_reduce(
                    out=zz[:, tch:tch + 1], in_=ee[:], axis=AX.X, op=ALU.add),
                    reads=[ee], writes=[zz])
                s.op("vector", lambda e, tch=tch: e.reciprocal(
                    out=zz[:, tch:tch + 1], in_=zz[:, tch:tch + 1]), reads=[zz], writes=[zz])
                s.op("vector", lambda e, tch=tch: e.tensor_scalar(
                    out=g_tm[:, tch * 4:(tch + 1) * 4], in0=ee[:],
                    scalar1=zz[:, tch:tch + 1], scalar2=None, op0=ALU.mult),
                    reads=[ee, zz], writes=[g_tm])
            for tch in range(8):
                ac = acc[0]
                xt = xtm_t[0]
                ot = out_t[0]
                s.op("sync", lambda e, tch=tch, xt=xt: e.dma_start(
                    out=xt[:], in_=d_xtm[tch * 128:(tch + 1) * 128, :]),
                    reads=[d_xtm], writes=[xt], dma=True)
                s.op("gpsimd", lambda e, ac=ac: e.memset(ac[:], 0.0), writes=[ac])
                for ex in range(4):
                    for oc in range(8):
                        psx = b.take_psum()
                        s.op("tensor", lambda e, ex=ex, tch=tch, oc=oc, ps=psx: e.matmul(
                            ps[:], mid_b[ex][:, tch * 128:(tch + 1) * 128],
                            bt_b[ex][:, oc * 512:(oc + 1) * 512], start=True, stop=True),
                            reads=[mid_b[ex], bt_b[ex]], writes=[psx])
                        s.op("vector", lambda e, ex=ex, tch=tch, oc=oc, ps=psx, ac=ac:
                             e.scalar_tensor_tensor(
                             out=ac[:, oc * 512:(oc + 1) * 512], in0=ps[:],
                             scalar=g_tm[:, tch * 4 + ex:tch * 4 + ex + 1],
                             in1=ac[:, oc * 512:(oc + 1) * 512],
                             op0=ALU.mult, op1=ALU.add),
                             reads=[psx, g_tm, ac], writes=[ac])
                s.op("vector", lambda e, ac=ac, xt=xt, ot=ot: e.scalar_tensor_tensor(
                    out=ot[:], in0=ac[:], scalar=MSCALE, in1=xt[:],
                    op0=ALU.mult, op1=ALU.add), reads=[ac, xt], writes=[ot])
                s.op("gpsimd", lambda e, tch=tch, ot=ot: e.dma_start(
                    out=d_out[tch * 128:(tch + 1) * 128, :], in_=ot[:]),
                    reads=[ot], writes=[d_out], dma=True)
            s.finalize(block, final_waits=[s.last_writer[id(d_out)]])
    return b.nc


def run_moe(x_fm, A, Bm, Wr):
    nc = build_moe_launch()
    aT = np.ascontiguousarray(A.transpose(0, 2, 1))
    bT = np.ascontiguousarray(Bm.transpose(0, 2, 1))
    wrT = np.ascontiguousarray(Wr.T)
    in_maps = []
    for c in range(NCORES):
        sl_ = np.ascontiguousarray(x_fm[:, c * TPC:(c + 1) * TPC])
        in_maps.append({"x": sl_, "xtm": np.ascontiguousarray(sl_.T),
                        "aT": aT, "bT": bT, "wrT": wrT})
    res = run_bass_kernel_spmd(nc, in_maps, core_ids=list(range(NCORES)))
    out = np.concatenate([res.results[c]["out"].T for c in range(NCORES)], axis=1)
    return out



def kernel(**inputs):
    g = inputs
    x_fm = np.ascontiguousarray(
        np.asarray(g["L0_attn_a_hidden_states"], dtype=np.float32).reshape(TOK, E).T)
    qk = run_qkvh(x_fm, g["L0_attn_a_Wq"], g["L0_attn_a_Wk"], g["L0_attn_a_Wv"],
                  g["L0_attn_a_hash_proj1"], g["L0_attn_a_hash_bias1"],
                  g["L0_attn_a_hash_proj2"], g["L0_attn_a_hash_bias2"])
    ao = run_attn0(qk["qr"], qk["qh"], qk["kr"], qk["kh"], qk["v"])
    x_fm = run_proj_resid(x_fm, ao, np.ascontiguousarray(g["L0_attn_a_Wo"].T), None, "wo0")
    wintT = np.ascontiguousarray(g["L0_ffn_weight_quantized"].T)
    x_fm = run_ffn(x_fm, wintT, g["L0_ffn_bias"], g["L0_ffn_scale"], g["L0_ffn_zero_point"])
    x_fm = run_mamba(x_fm, None, g["L0_attn_b_conv_weight"], g["L0_attn_b_conv_bias"],
                     g["L0_attn_b_dt_bias"], g["L0_attn_b_A_log"], g["L0_attn_b_D"],
                     g["L0_attn_b_norm_weight"], g["L0_attn_b_out_proj_w"],
                     g["L0_attn_b_in_proj_w"])
    x_fm = run_ffn(x_fm, wintT, g["L0_ffn_bias"], g["L0_ffn_scale"], g["L0_ffn_zero_point"])
    x_fm = run_l1_attn(x_fm, g["L1_attn_a_Wqkv"], g["L1_attn_a_bqkv"],
                       g["L1_attn_a_Wo"], g["L1_attn_a_bo"])
    x_fm = run_moe(x_fm, g["L1_ffn_A"], g["L1_ffn_B"], g["L1_ffn_Wr"])
    x_fm = run_mamba(x_fm, None, g["L1_attn_b_conv_weight"], g["L1_attn_b_conv_bias"],
                     g["L1_attn_b_dt_bias"], g["L1_attn_b_A_log"], g["L1_attn_b_D"],
                     g["L1_attn_b_norm_weight"], g["L1_attn_b_out_proj_w"],
                     g["L1_attn_b_in_proj_w"])
    x_fm = run_moe(x_fm, g["L1_ffn_A"], g["L1_ffn_B"], g["L1_ffn_Wr"])
    return np.ascontiguousarray(x_fm.T).reshape(B, S, E).astype(np.float32)
